# version 24
# baseline (speedup 1.0000x reference)
import numpy as np
from contextlib import ExitStack

import concourse.bass as bass
import concourse.mybir as mybir
from concourse.bass_utils import run_bass_kernel_spmd

F32 = mybir.dt.float32
BF16 = mybir.dt.bfloat16
AF = mybir.ActivationFunctionType
ALU = mybir.AluOpType
AX = mybir.AxisListType

N_CORES = 8
NTOK = 2048
D = 1024
NT = NTOK // 128
ND = D // 128


class Prog:
    CHAINS = ("t", "v", "s", "g", "y")

    def __init__(self, nc, es):
        self.nc = nc
        self.es = es
        self.ops = {e: [] for e in self.CHAINS}
        self.cnt = {e: 0 for e in self.CHAINS}
        self.sems = {}
        for e in self.CHAINS:
            self.sems["ch_" + e] = es.enter_context(nc.semaphore("ch_" + e))
        self.dma_issued = {}
        self.last_write = {}
        self.readers = {}
        self.clock = {e: {} for e in self.CHAINS}
        self.snap = {}

    def dma_sem(self, name):
        if name not in self.dma_issued:
            self.sems[name] = self.es.enter_context(self.nc.semaphore(name))
            self.dma_issued[name] = 0
        return name

    def _waits_for(self, e, reads, writes):
        toks = []
        for k in reads:
            t = self.last_write.get(k)
            if t:
                toks.append(t)
        for k in writes:
            t = self.last_write.get(k)
            if t:
                toks.append(t)
            toks.extend(self.readers.get(k, ()))
        need = {}
        clk = self.clock[e]
        for (sn, v) in toks:
            if sn in self.dma_issued:
                v = 16 * self.dma_issued[sn]
            if e == "t" and sn == "ch_t":
                continue
            if clk.get(sn, -1) < v:
                need[sn] = max(need.get(sn, 0), v)
                s2 = self.snap.get((sn, v))
                if s2:
                    for sn2, v2 in s2.items():
                        if clk.get(sn2, -1) < v2:
                            clk[sn2] = v2
        return need

    def _record(self, e, reads, writes, token):
        for k in writes:
            self.last_write[k] = token
            self.readers[k] = []
        for k in reads:
            self.readers.setdefault(k, []).append(token)

    def op(self, e, fn, reads=(), writes=(), skip_dep_check=False, inc=1):
        reads = norm_keys(reads)
        writes = norm_keys(writes)
        need = {} if skip_dep_check else self._waits_for(e, reads, writes)
        self.cnt[e] += 1
        myval = self.cnt[e]
        mysem = "ch_" + e
        sems = self.sems

        def emit(eng, need=need, fn=fn, mysem=mysem):
            for sn in sorted(need):
                eng.wait_ge(sems[sn], need[sn])
            ins = fn(eng)
            ins.then_inc(sems[mysem], 1)
        self.ops[e].append(emit)
        for sn, v in need.items():
            self.clock[e][sn] = max(self.clock[e].get(sn, -1), v)
        token = (mysem, myval)
        snap = dict(self.clock[e])
        snap[mysem] = myval - 1
        self.snap[token] = snap
        self._record(e, reads, writes, token)
        return token

    def dma(self, e, semname, fn, reads=(), writes=()):
        self.dma_sem(semname)
        reads = norm_keys(reads)
        writes = norm_keys(writes)
        need = self._waits_for(e, reads, writes)
        self.dma_issued[semname] += 1
        myval = 16 * self.dma_issued[semname]
        sems = self.sems

        def emit(eng, need=need, fn=fn, semname=semname):
            for sn in sorted(need):
                eng.wait_ge(sems[sn], need[sn])
            fn(eng).then_inc(sems[semname], 16)
        self.ops[e].append(emit)
        for sn, v in need.items():
            self.clock[e][sn] = max(self.clock[e].get(sn, -1), v)
        token = (semname, myval)
        self.snap[token] = dict(self.clock[e])
        self._record(e, reads, writes, token)
        return token

    def dma_batch(self, e, semname, fns, reads=(), writes=()):
        self.dma_sem(semname)
        reads = norm_keys(reads)
        writes = norm_keys(writes)
        need = self._waits_for(e, reads, writes)
        sems = self.sems

        def emit(eng, need=need, fns=fns, semname=semname):
            for sn in sorted(need):
                eng.wait_ge(sems[sn], need[sn])
            for fn in fns:
                fn(eng).then_inc(sems[semname], 16)
        self.ops[e].append(emit)
        for sn, v in need.items():
            self.clock[e][sn] = max(self.clock[e].get(sn, -1), v)
        self.dma_issued[semname] += len(fns)
        token = (semname, 16 * self.dma_issued[semname])
        self.snap[token] = dict(self.clock[e])
        self._record(e, reads, writes, token)
        return token

    def full_barrier(self):
        snap = [("ch_" + e, self.cnt[e]) for e in self.CHAINS]
        snap += [(s, 16 * n) for s, n in self.dma_issued.items()]
        for e in self.CHAINS:
            for sn, v in snap:
                if sn == "ch_" + e or v <= 0:
                    continue
                if self.clock[e].get(sn, -1) < v:
                    self.barrier_wait(e, sn, v)
                    self.clock[e][sn] = v

    def barrier_wait(self, e, semname, val):
        sems = self.sems

        def emit(eng):
            eng.wait_ge(sems[semname], val)
        self.ops[e].append(emit)

    def finalize(self, blk):
        prog = self

        def mk(e):
            def run(eng):
                for f in prog.ops[e]:
                    f(eng)
            return run
        blk.tensor(mk("t"))
        blk.vector(mk("v"))
        blk.scalar(mk("s"))
        blk.gpsimd(mk("g"))
        blk.sync(mk("y"))


def key(t, tag=""):
    return t.tensor.name if isinstance(t, bass.AP) else t.name


STREAM_RG = 512


def ap_keys(ap):
    if not isinstance(ap, bass.AP):
        return [ap]
    name = ap.tensor.name
    if (name.startswith("xt") or name.startswith("xb")) and name[2:].isdigit():
        rs = int(ap.tensor.shape[-1])
        off = int(ap.offset) % rs
        inner = ap.ap[-1]
        assert int(inner[0]) == 1
        lo, hi = off, off + int(inner[1])
        return [f"{name}@r{r}" for r in range(lo // STREAM_RG, (hi + STREAM_RG - 1) // STREAM_RG)]
    return [name]


def norm_keys(lst):
    out = []
    for x in lst:
        out.extend(ap_keys(x))
    return out


def w_matmul(P, out_ps, lhsT, rhs, start, stop, pskey, extra_reads=()):
    reads = [lhsT, rhs] + list(extra_reads)
    if start:
        writes = [pskey]
        tok = P.op("t", lambda t: t.matmul(out_ps, lhsT, rhs, start=start, stop=stop),
                   reads=reads, writes=writes)
    else:
        tok = P.op("t", lambda t: t.matmul(out_ps, lhsT, rhs, start=start, stop=stop),
                   reads=reads, writes=[])
        if stop:
            P.last_write[pskey] = tok
            P.readers[pskey] = []
    return tok


def w_transpose(P, out_ps, in_sb, ident, pskey):
    return P.op("t", lambda t: t.transpose(out_ps, in_sb, ident),
                reads=[in_sb, ident], writes=[pskey])


def w_copy(P, e, dst, src, dstkey=None, srckey=None):
    if e == "s":
        return P.op(e, lambda x: x.copy(dst, src),
                    reads=[srckey if srckey is not None else src],
                    writes=[dstkey if dstkey is not None else dst])
    return P.op(e, lambda x: x.tensor_copy(dst, src),
                reads=[srckey if srckey is not None else src],
                writes=[dstkey if dstkey is not None else dst])


def w_tt(P, e, op, dst, a, b, dkey=None, rkeys=None):
    return P.op(e, lambda x: x.tensor_tensor(dst, a, b, op),
                reads=rkeys if rkeys is not None else [a, b],
                writes=[dkey if dkey is not None else dst])


def w_ts(P, e, dst, a, s1, s2, op0, op1=ALU.bypass, dkey=None, rkeys=None):
    if s2 is None:
        return P.op(e, lambda x: x.tensor_single_scalar(dst, a, s1, op0),
                    reads=rkeys if rkeys is not None else [a] + ([s1] if isinstance(s1, bass.AP) else []),
                    writes=[dkey if dkey is not None else dst])
    return P.op(e, lambda x: x.tensor_scalar(dst, a, s1, s2, op0, op1),
                reads=rkeys if rkeys is not None else [a] + [s for s in (s1, s2) if isinstance(s, bass.AP)],
                writes=[dkey if dkey is not None else dst])


def w_stt(P, e, dst, a, scalar, b, op0, op1, dkey=None, rkeys=None):
    return P.op(e, lambda x: x.scalar_tensor_tensor(dst, a, scalar, b, op0, op1),
                reads=rkeys if rkeys is not None else [a, b] + ([scalar] if isinstance(scalar, bass.AP) else []),
                writes=[dkey if dkey is not None else dst])


def w_act(P, dst, src, func, bias=0.0, scale=1.0, accum=None, dkey=None, rkeys=None):
    writes = [dkey if dkey is not None else dst] + ([accum] if accum is not None else [])
    reads = rkeys if rkeys is not None else [src] + [x for x in (bias, scale) if isinstance(x, bass.AP)]
    return P.op("s", lambda s: s.activation(dst, src, func, bias=bias, scale=scale,
                                            accum_out=accum),
                reads=reads, writes=writes)


def w_reduce(P, e, dst, src, op, dkey=None, srckey=None, negate=None):
    return P.op(e, lambda x: x.tensor_reduce(dst, src, AX.X, op, negate=negate),
                reads=[srckey if srckey is not None else src],
                writes=[dkey if dkey is not None else dst])


def w_memset(P, e, dst, val, dkey=None):
    return P.op(e, lambda x: x.memset(dst, val),
                reads=[], writes=[dkey if dkey is not None else dst])


def w_recip(P, dst, src, dkey=None, srckey=None):
    return P.op("v", lambda v: v.reciprocal(dst, src),
                reads=[srckey if srckey is not None else src],
                writes=[dkey if dkey is not None else dst])


def w_dma(P, e, semname, dst, src, dstkey=None, srckey=None):
    if semname is None:
        semname = "dm_" + key(dst)
    return P.dma(e, semname, lambda x: x.dma_start(out=dst, in_=src),
                 reads=[srckey if srckey is not None else src],
                 writes=[dstkey if dstkey is not None else dst])


def w_dma_batch(P, e, pairs, dstkey, srckeys, semname=None):
    if semname is None:
        semname = "dm_" + dstkey
    fns = [(lambda x, d=d, s=s: x.dma_start(out=d, in_=s)) for d, s in pairs]
    return P.dma_batch(e, semname, fns, reads=srckeys, writes=[dstkey])


class Ctx:
    pass


def make_ctx(nc, es, P, need_inputs):
    c = Ctx()
    c.nc, c.es, c.P = nc, es, P
    c.inp = {}
    for nm, shp, dt in need_inputs:
        c.inp[nm] = nc.dram_tensor(nm, shp, dt, kind="ExternalInput").ap()
    c.out = nc.dram_tensor("out", [D, NTOK], F32, kind="ExternalOutput").ap()

    c._scopes = [es]

    def SB(n, s, d):
        return c._scopes[-1].enter_context(nc.sbuf_tensor(n, s, d))

    def PS(n, s, d):
        return c._scopes[-1].enter_context(nc.psum_tensor(n, s, d))

    def push_scope():
        c._scopes.append(ExitStack())

    def pop_scope():
        c.P.full_barrier()
        c._scopes.pop().close()
    c.SB, c.PS = SB, PS
    c.push_scope, c.pop_scope = push_scope, pop_scope
    c.xt = [SB(f"xt{i}", [128, NTOK], F32) for i in range(ND)]
    c.xb = [SB(f"xb{i}", [128, NTOK], BF16) for i in range(ND)]
    c.onec_f = SB("s_c_onec_f", [128, 1], F32)
    c.onec_bf = SB("s_c_onec_bf", [128, 1], BF16)
    c.one1_f = SB("s_c_one1_f", [1, 128], F32)
    c.one1_bf = SB("s_c_one1_bf", [1, 128], BF16)
    c.ps = [PS(f"psm{i}", [128, 512], F32) for i in range(4)]
    c.psx = [PS(f"psx{i}", [128, 512], F32) for i in range(2)]
    c.pst = PS("pst", [128, 1024], BF16)
    c.psr = PS("psr", [128, 512], F32)
    for t, nm in [(c.onec_f, "c_onec_f"), (c.one1_f, "c_one1_f"),
                  (c.one1_bf, "c_one1_bf"), (c.onec_bf, "c_onec_bf")]:
        w_dma(P, "y", None, t[:], c.inp[nm])
    return c


def load_const(c, sbuf_name, inp_name):
    ap = c.inp[inp_name]
    t = c.SB(sbuf_name, list(ap.shape), ap.dtype)
    w_dma(c.P, "y", None, t[:], ap)
    return t


def const_inputs_spec():
    return [("c_id_bf", [128, 128], BF16), ("c_id_f", [128, 128], F32),
            ("c_onec_f", [128, 1], F32), ("c_onec_bf", [128, 1], BF16),
            ("c_oner_f", [1, 512], F32),
            ("c_one1_f", [1, 128], F32), ("c_one1_bf", [1, 128], BF16),
            ("c_sl_f", [128, 128], F32), ("c_tu_f", [128, 128], F32),
            ("c_slneg", [128, 128], F32), ("c_suneg", [128, 128], F32),
            ("c_cm01", [128, 896], BF16)]


def const_inputs_host():
    import ml_dtypes
    BF = ml_dtypes.bfloat16
    idm = np.eye(128, dtype=np.float32)
    sl = np.tril(np.ones((128, 128), np.float32), -1)
    tu = np.triu(np.ones((128, 128), np.float32), 0)
    f = np.arange(896)[None, :]
    p = np.arange(128)[:, None]
    cm = (f >= p + 384).astype(np.float32)
    slneg = -np.tril(np.ones((128, 128), np.float32), -1)
    suneg = -np.triu(np.ones((128, 128), np.float32), 1)
    return {"c_id_bf": idm.astype(BF), "c_id_f": idm,
            "c_slneg": slneg, "c_suneg": suneg,
            "c_onec_f": np.ones((128, 1), np.float32),
            "c_onec_bf": np.ones((128, 1)).astype(BF),
            "c_oner_f": np.ones((1, 512), np.float32),
            "c_one1_f": np.ones((1, 128), np.float32),
            "c_one1_bf": np.ones((1, 128)).astype(BF),
            "c_sl_f": sl, "c_tu_f": tu, "c_cm01": cm.astype(BF)}


def emit_load_stream(c):
    P = c.P
    for i in range(ND):
        w_dma(P, "y", None, c.xt[i][:], c.inp["xin"][i * 128:(i + 1) * 128, :])
        w_copy(P, "v" if i % 2 == 0 else "s", c.xb[i][:], c.xt[i][:])


def emit_store_stream(c):
    P = c.P
    for i in range(ND):
        w_dma(P, "y", "so", c.out[i * 128:(i + 1) * 128, :], c.xt[i][:])
    P.barrier_wait("y", "so", 16 * P.dma_issued["so"])


def load_weight_bf16(c, dst_sb, dram_ap, rows, cols):
    w_dma(c.P, "g", None, dst_sb[:, 0:(rows // 128) * cols], dram_ap,
          dstkey=key(dst_sb), srckey=key(dram_ap))


def ln_update(c, res, chunk, scr, W=512, col0=None):
    P = c.P
    if col0 is None:
        col0 = chunk * W
    cs = slice(col0, col0 + W)
    rap = [r if isinstance(r, bass.AP) else r[:] for r in res]
    for d in range(ND):
        w_matmul(P, c.psr[0:1, 0:W], c.onec_f[:], rap[d], d == 0, d == ND - 1, "psr")
    w_copy(P, "v", scr["rowm"][:, 0:W], c.psr[0:1, 0:W], dstkey=key(scr["rowm"]), srckey="psr")
    for d in range(ND):
        w_act(P, scr["sq"][:, 0:W], rap[d], AF.Square, dkey=key(scr["sq"]), rkeys=[key(rap[d])])
        w_matmul(P, c.psr[0:1, 0:W], c.onec_bf[:], scr["sq"][:, 0:W], d == 0, d == ND - 1, "psr")
    w_copy(P, "v", scr["rowq"][:, 0:W], c.psr[0:1, 0:W], dstkey=key(scr["rowq"]), srckey="psr")
    w_ts(P, "v", scr["rowm"][:, 0:W], scr["rowm"][:, 0:W], 1.0 / D, None, ALU.mult,
         dkey=key(scr["rowm"]), rkeys=[key(scr["rowm"])])
    w_ts(P, "v", scr["rowq"][:, 0:W], scr["rowq"][:, 0:W], 1.0 / D, None, ALU.mult,
         dkey=key(scr["rowq"]), rkeys=[key(scr["rowq"])])
    w_tt(P, "v", ALU.mult, scr["rowa"][:, 0:W], scr["rowm"][:, 0:W], scr["rowm"][:, 0:W],
         dkey=key(scr["rowa"]), rkeys=[key(scr["rowm"])])
    w_tt(P, "v", ALU.subtract, scr["rowq"][:, 0:W], scr["rowq"][:, 0:W], scr["rowa"][:, 0:W],
         dkey=key(scr["rowq"]), rkeys=[key(scr["rowq"]), key(scr["rowa"])])
    w_ts(P, "v", scr["rowq"][:, 0:W], scr["rowq"][:, 0:W], 1e-5, None, ALU.add,
         dkey=key(scr["rowq"]), rkeys=[key(scr["rowq"])])
    P.op("s", lambda s: s.activation(scr["rowa"][:, 0:W], scr["rowq"][:, 0:W], AF.Sqrt),
         reads=[key(scr["rowq"])], writes=[key(scr["rowa"])])
    P.op("v", lambda v: v.reciprocal(scr["rowq"][:, 0:W], scr["rowa"][:, 0:W]),
         reads=[key(scr["rowa"])], writes=[key(scr["rowq"])])
    w_tt(P, "v", ALU.mult, scr["rowa"][:, 0:W], scr["rowm"][:, 0:W], scr["rowq"][:, 0:W],
         dkey=key(scr["rowa"]), rkeys=[key(scr["rowm"]), key(scr["rowq"])])
    w_ts(P, "v", scr["rowa"][:, 0:W], scr["rowa"][:, 0:W], -1.0, None, ALU.mult,
         dkey=key(scr["rowa"]), rkeys=[key(scr["rowa"])])
    w_matmul(P, c.psx[0][:, 0:W], c.one1_f[:], scr["rowq"][:, 0:W], True, True, "psx0")
    w_matmul(P, c.psx[1][:, 0:W], c.one1_f[:], scr["rowa"][:, 0:W], True, True, "psx1")
    for d in range(ND):
        w_tt(P, "v", ALU.mult, scr["t1"][:, 0:W], rap[d], c.psx[0][:, 0:W],
             dkey=key(scr["t1"]), rkeys=[key(rap[d]), "psx0"])
        P.op("v", lambda v, d=d: v.tensor_tensor(c.xt[d][:, cs], scr["t1"][:, 0:W],
             c.psx[1][:, 0:W], ALU.add), reads=[key(scr["t1"]), "psx1"],
             writes=[c.xt[d][:, cs]])
        P.op("g", lambda g, d=d: g.tensor_copy(c.xb[d][:, cs], c.xt[d][:, cs]),
             reads=[c.xt[d][:, cs]], writes=[c.xb[d][:, cs]])


def ln_scratch(c, pfx, t1=None, W=512):
    return {"sq": c.SB(pfx + "_sq", [128, W], BF16),
            "rowm": c.SB(pfx + "_rowm", [1, W], F32),
            "rowq": c.SB(pfx + "_rowq", [1, W], F32),
            "rowa": c.SB(pfx + "_rowa", [1, W], F32),
            "t1": t1 if t1 is not None else c.SB(pfx + "_t1", [128, W], F32)}


def emit_l1f(c):
    P = c.P
    SB = lambda n, s, d: c.es.enter_context(c.nc.sbuf_tensor(n, s, d))
    wr = SB("f_wr", [128, 8 * D], BF16)
    wk = SB("f_wk", [128, 8 * D], BF16)
    wv = SB("f_wv", [128, 8 * D], BF16)
    mur = SB("f_mur", [128, 8], F32)
    muk = SB("f_muk", [128, 8], F32)
    load_weight_bf16(c, wr, c.inp["wrT"], D, D)
    load_weight_bf16(c, wk, c.inp["wkT"], D, D)
    load_weight_bf16(c, wv, c.inp["wvT"], D, D)
    w_dma(P, "y", None, mur[:], c.inp["mur"])
    w_dma(P, "y", None, muk[:], c.inp["muk"])

    QT = NTOK // 512
    diff = SB("f_diff", [128, 512], F32)
    xr = [SB(f"f_xr{d}", [128, 512], BF16) for d in range(ND)]
    xk = [SB(f"f_xk{d}", [128, 512], BF16) for d in range(ND)]
    rbuf = [SB(f"f_r{d}", [128, 512], BF16) for d in range(ND)]
    krel = [SB("f_kr", [128, 512], BF16)] * ND
    kact = [SB(f"f_ka{d}", [128, 512], BF16) for d in range(ND)]
    res = [SB(f"f_res{d}", [128, 512], F32) for d in range(ND)]
    pcol = [SB(f"f_pc{d}", [128, 1], F32) for d in range(ND)]
    lsc = ln_scratch(c, "f_ln", t1=diff)

    for q in range(QT):
        t0 = q * 512
        for d in range(ND):
            w_tt(P, "v", ALU.subtract, diff[:, 1:512], c.xt[d][:, t0 + 1:t0 + 512],
                 c.xt[d][:, t0:t0 + 511], dkey=key(diff),
                 rkeys=[c.xt[d][:, t0:t0 + 512]])
            w_stt(P, "v", xr[d][:, 1:512], diff[:, 1:512], mur[:, d:d + 1],
                  c.xt[d][:, t0:t0 + 511], ALU.mult, ALU.add,
                  dkey=key(xr[d]), rkeys=[key(diff), key(mur), c.xt[d][:, t0:t0 + 512]])
            w_stt(P, "v", xk[d][:, 1:512], diff[:, 1:512], muk[:, d:d + 1],
                  c.xt[d][:, t0:t0 + 511], ALU.mult, ALU.add,
                  dkey=key(xk[d]), rkeys=[key(diff), key(muk), c.xt[d][:, t0:t0 + 512]])
            if q == 0:
                w_ts(P, "v", xr[d][:, 0:1], c.xt[d][:, 0:1], mur[:, d:d + 1], None,
                     ALU.mult, dkey=key(xr[d]), rkeys=[c.xt[d][:, 0:1], key(mur)])
                w_ts(P, "v", xk[d][:, 0:1], c.xt[d][:, 0:1], muk[:, d:d + 1], None,
                     ALU.mult, dkey=key(xk[d]), rkeys=[c.xt[d][:, 0:1], key(muk)])
            else:
                w_tt(P, "v", ALU.subtract, diff[:, 0:1], c.xt[d][:, t0:t0 + 1],
                     pcol[d][:], dkey=key(diff), rkeys=[c.xt[d][:, t0:t0 + 1], key(pcol[d])])
                w_stt(P, "v", xr[d][:, 0:1], diff[:, 0:1], mur[:, d:d + 1],
                      pcol[d][:], ALU.mult, ALU.add,
                      dkey=key(xr[d]), rkeys=[key(diff), key(mur), key(pcol[d])])
                w_stt(P, "v", xk[d][:, 0:1], diff[:, 0:1], muk[:, d:d + 1],
                      pcol[d][:], ALU.mult, ALU.add,
                      dkey=key(xk[d]), rkeys=[key(diff), key(muk), key(pcol[d])])
            w_copy(P, "v", pcol[d][:], c.xt[d][:, t0 + 511:t0 + 512],
                   srckey=c.xt[d][:, t0 + 511:t0 + 512])
        for n in range(ND):
            ps = c.ps[n % 2]
            pk = f"psm{n % 2}"
            for d in range(ND):
                w_matmul(P, ps[:, 0:512], wr[:, d * D + n * 128:d * D + n * 128 + 128],
                         xr[d][:], d == 0, d == ND - 1, pk)
            w_act(P, rbuf[n][:], ps[:, 0:512], AF.Sigmoid, rkeys=[pk])
            ps2 = c.ps[2 + n % 2]
            pk2 = f"psm{2 + n % 2}"
            for d in range(ND):
                w_matmul(P, ps2[:, 0:512], wk[:, d * D + n * 128:d * D + n * 128 + 128],
                         xk[d][:], d == 0, d == ND - 1, pk2)
            P.op("v", lambda v, n=n, ps2=ps2: v.tensor_scalar_max(krel[n][:], ps2[:, 0:512], 0.0),
                 reads=[pk2], writes=[key(krel[n])])
            P.op("s", lambda s, n=n: s.activation(kact[n][:], krel[n][:], AF.Square),
                 reads=[key(krel[n])], writes=[key(kact[n])])
        for n in range(ND):
            ps = c.ps[n % 2]
            pk = f"psm{n % 2}"
            for d in range(ND):
                w_matmul(P, ps[:, 0:512], wv[:, d * D + n * 128:d * D + n * 128 + 128],
                         kact[d][:], d == 0, d == ND - 1, pk)
            w_stt(P, "v", res[n][:], ps[:, 0:512], 1.0, rbuf[n][:], ALU.mult, ALU.mult,
                  dkey=key(res[n]), rkeys=[pk, key(rbuf[n])])
            w_tt(P, "v", ALU.add, res[n][:], res[n][:], c.xt[n][:, t0:t0 + 512],
                 rkeys=[key(res[n]), c.xt[n][:, t0:t0 + 512]])
        ln_update(c, res, q, lsc)


def emit_l1a(c):
    P = c.P
    SB = c.SB
    NH, DH = 16, 64
    wslot = SB("a_wslot", [128, 8 * D], BF16)
    V65 = [SB(f"a_v{t}", [128, 16 * 65], BF16) for t in range(NT)]
    KT = [SB(f"a_kt{j}", [128, NTOK], BF16) for j in range(8)]
    QTc = [SB(f"a_qt{j}", [128, 512], BF16) for j in range(8)]
    OTc = [SB(f"a_ot{j}", [128, 512], BF16) for j in range(8)]
    PP = [SB(f"a_p{i}", [128, 512], BF16) for i in range(8)]
    sq = PP[7]
    onec_bf = SB("a_onec_bf", [128, 1], BF16)
    negb = SB("a_negb", [128, 8], F32)
    qm = SB("a_qm", [1, 8], F32)
    km = SB("a_km", [1, 8], F32)
    s11 = SB("a_s11", [1, 1], F32)
    denb = SB("a_denb", [64, 512], BF16)
    denr_bf = SB("a_denr", [1, 512], BF16)
    res = [PP[d][:].bitcast(F32)[:, 0:128] for d in range(ND)]
    res_keys = [key(PP[d]) for d in range(ND)]
    lsc = ln_scratch(c, "a_ln", W=128)
    cm01 = load_const(c, "s_cm01", "c_cm01")
    P.op("v", lambda v: v.memset(onec_bf[:], 1.0), writes=[key(onec_bf)])

    load_weight_bf16(c, wslot, c.inp["avT"], D, D)
    for t in range(NT):
        P.op("v", lambda v, t=t: v.memset(
            V65[t][:].rearrange("p (h c) -> p h c", c=65)[:, :, 64:65], 1.0),
            writes=[key(V65[t])])
        for nch in range(2):
            ps = c.ps[nch % 2]
            pk = f"psm{nch % 2}"
            for d in range(ND):
                w_matmul(P, ps[:, 0:512], c.xb[d][:, t * 128:(t + 1) * 128],
                         wslot[:, d * D + nch * 512:d * D + nch * 512 + 512],
                         d == 0, d == ND - 1, pk)
            for hh in range(8):
                h = nch * 8 + hh
                w_copy(P, "v" if hh % 2 == 0 else "s",
                       V65[t][:, h * 65:h * 65 + 64], ps[:, hh * 64:hh * 64 + 64],
                       dstkey=key(V65[t]), srckey=pk)

    load_weight_bf16(c, wslot, c.inp["akT"], D, D)
    for j in range(8):
        for ch in range(4):
            ps = c.ps[ch % 2]
            pk = f"psm{ch % 2}"
            for d in range(ND):
                w_matmul(P, ps[:, 0:512],
                         wslot[:, d * D + j * 128:d * D + j * 128 + 128],
                         c.xb[d][:, ch * 512:ch * 512 + 512],
                         d == 0, d == ND - 1, pk)
            w_copy(P, "v" if ch % 2 == 0 else "s", KT[j][:, ch * 512:ch * 512 + 512],
                   ps[:, 0:512], dstkey=key(KT[j]), srckey=pk)
        for ch in range(4):
            w_act(P, sq[:], KT[j][:, ch * 512:ch * 512 + 512], AF.Square,
                  dkey=key(sq), rkeys=[key(KT[j])])
            w_matmul(P, c.psr[0:1, 0:512], onec_bf[:], sq[:], True, True, "psr")
            w_reduce(P, "v", s11[:], c.psr[0:1, 0:512], ALU.max, srckey="psr")
            if ch == 0:
                w_copy(P, "v", km[:, j:j + 1], s11[:], dstkey=key(km), srckey=key(s11))
            else:
                P.op("v", lambda v, j=j: v.tensor_tensor(
                    km[:, j:j + 1], km[:, j:j + 1], s11[:], ALU.max),
                    reads=[key(km), key(s11)], writes=[key(km)])

    for qc in range(4):
        load_weight_bf16(c, wslot, c.inp["aqT"], D, D)
        for j in range(8):
            ps = c.ps[j % 2]
            pk = f"psm{j % 2}"
            for d in range(ND):
                w_matmul(P, ps[:, 0:512],
                         wslot[:, d * D + j * 128:d * D + j * 128 + 128],
                         c.xb[d][:, qc * 512:qc * 512 + 512],
                         d == 0, d == ND - 1, pk)
            w_copy(P, "v" if j % 2 == 0 else "s", QTc[j][:], ps[:, 0:512], srckey=pk)
            w_act(P, sq[:], QTc[j][:], AF.Square, dkey=key(sq), rkeys=[key(QTc[j])])
            w_matmul(P, c.psr[0:1, 0:512], onec_bf[:], sq[:], True, True, "psr")
            w_reduce(P, "v", s11[:], c.psr[0:1, 0:512], ALU.max, srckey="psr")
            w_tt(P, "v", ALU.add, s11[:], s11[:], km[:, j:j + 1],
                 dkey=key(s11), rkeys=[key(s11), key(km)])
            w_ts(P, "v", s11[:], s11[:], -1.0 / 16.0, None, ALU.mult)
            w_matmul(P, c.psx[0][:, 0:1], c.one1_f[:], s11[:], True, True, "psx0")
            w_copy(P, "v", negb[:, j:j + 1], c.psx[0][:, 0:1], dstkey=key(negb), srckey="psx0")

        load_weight_bf16(c, wslot, c.inp["aoT"], D, D)

        AVB = [(c.psx[0], "psx0"), (c.ps[0], "psm0"), (c.psx[1], "psx1"), (c.ps[1], "psm1")]

        def denscale(h):
            j = h // 2
            oh = (h % 2) * 64
            pav, pkav = AVB[h % 4]
            P.op("v", lambda v, pav=pav: v.reciprocal(c.psr[64:65, 0:512],
                 pav[64:65, 0:512]), reads=[pkav], writes=["psr"])
            P.op("v", lambda v: v.tensor_copy(denr_bf[:], c.psr[64:65, 0:512]),
                 reads=["psr"], writes=[key(denr_bf)])
            w_matmul(P, c.psr[0:64, 0:512], c.one1_bf[:, 0:64], denr_bf[:], True, True, "psr")
            w_copy(P, "s", denb[:], c.psr[0:64, 0:512], srckey="psr")
            w_stt(P, "v", OTc[j][oh:oh + 64, :],
                  pav[0:64, 0:512], 1.0, denb[:], ALU.mult, ALU.mult,
                  dkey=key(OTc[j]), rkeys=[pkav, key(denb)])

        kbmax = qc * 4 + 3
        for hp in range(8):
            h0, h1 = 2 * hp, 2 * hp + 1
            j = hp // 1 // 2 * 0 + hp // 2
            for kb in range(kbmax + 1):
                for hi, h in enumerate((h0, h1)):
                    jt = h // 2
                    oh = (h % 2) * 64
                    pst_b = c.ps[2 + hi]
                    pkst = f"psm{2 + hi}"
                    w_matmul(P, pst_b[:, 0:512],
                             KT[jt][oh:oh + 64, kb * 128:kb * 128 + 128],
                             QTc[jt][oh:oh + 64, :],
                             True, True, pkst)
                    pt = PP[4 * hi + kb % 4]
                    w_act(P, pt[:], pst_b[:, 0:512], AF.Exp,
                          bias=negb[:, jt:jt + 1], scale=0.125,
                          dkey=key(pt), rkeys=[pkst, key(negb)])
                    if kb >= qc * 4:
                        v_ = kb - qc * 4
                        w_tt(P, "v", ALU.mult, pt[:], pt[:],
                             cm01[:, 384 - 128 * v_:896 - 128 * v_],
                             rkeys=[key(pt), key(cm01)])
                    pav, pkav = AVB[h % 4]
                    w_matmul(P, pav[0:65, 0:512],
                             V65[kb][:, h * 65:h * 65 + 65], pt[:],
                             kb == 0, kb == kbmax, pkav)
            denscale(h0)
            denscale(h1)

        for sub in range(4):
            col0 = qc * 512 + sub * 128
            for d in range(ND):
                ps = c.ps[d % 2]
                pk = f"psm{d % 2}"
                for n in range(ND):
                    w_matmul(P, ps[:, 0:128],
                             wslot[:, n * D + d * 128:n * D + d * 128 + 128],
                             OTc[n][:, sub * 128:sub * 128 + 128],
                             n == 0, n == ND - 1, pk)
                P.op("v", lambda v, d=d, ps=ps, col0=col0: v.tensor_tensor(
                    res[d], ps[:, 0:128], c.xt[d][:, col0:col0 + 128], ALU.add),
                    reads=[pk, c.xt[d][:, col0:col0 + 128]], writes=[res_keys[d]])
            ln_update(c, res, None, lsc, W=128, col0=col0)


def emit_l0a(c):
    P = c.P
    SB = c.SB
    H, DK = 4, 256
    NC = 16
    id_bf = load_const(c, "s_id_bf", "c_id_bf")
    slneg = load_const(c, "s_slneg", "c_slneg")
    tu_f = load_const(c, "s_tu", "c_tu_f")
    gn_row = SB("d_gnrow", [1, DK], F32)
    w_dma(P, "y", None, gn_row[:], c.inp["dgn"])
    gnb = SB("d_gnb", [128, DK], F32)
    w_matmul(P, c.psx[0][:, 0:DK], c.one1_f[:], gn_row[:], True, True, "psx0")
    w_copy(P, "v", gnb[:], c.psx[0][:, 0:DK], srckey="psx0")
    bslot = SB("d_bslot", [128, 32], F32)
    w_dma(P, "y", None, bslot[:], c.inp["dwbeta"])
    beta = SB("d_beta", [128, 64], F32)
    for t in range(NT):
        for d in range(ND):
            w_matmul(P, c.psr[:, 0:4], c.xt[d][:, t * 128:(t + 1) * 128],
                     bslot[:, d * 4:(d + 1) * 4], d == 0, d == ND - 1, "psr")
        w_act(P, beta[:, t * 4:(t + 1) * 4], c.psr[:, 0:4], AF.Sigmoid,
              dkey=key(beta), rkeys=["psr"])

    OT = [SB(f"d_ot{j}", [128, NTOK], BF16) for j in range(8)]

    for pr in range(2):
        c.push_scope()
        QT = [[SB(f"d_qt{pr}_{hh}_{j}", [128, NTOK], BF16) for j in range(2)] for hh in range(2)]
        KT = [[SB(f"d_kt{pr}_{hh}_{j}", [128, NTOK], BF16) for j in range(2)] for hh in range(2)]
        ktok = [SB(f"d_ktok{pr}_{t}", [128, 2 * DK], BF16) for t in range(NT)]
        vtok = [SB(f"d_vtok{pr}_{t}", [128, 2 * DK], BF16) for t in range(NT)]
        nsq = SB(f"d_nsq{pr}", [128, 1], F32)

        bankP = [c.ps[1], c.ps[3]]
        bankX = [c.ps[0], c.ps[2]]
        kPa = ["pP0", "pP1"]
        kPb = ["pP0", "pP1"]
        kXa = ["pX0", "pX1"]
        kXb = ["pX0", "pX1"]

        c.push_scope()
        qslot = SB(f"d_qslot{pr}", [128, 8 * DK], BF16)
        kslot = SB(f"d_kslot{pr}", [128, 8 * DK], BF16)
        vslot = qslot
        qn = SB(f"d_qn{pr}", [128, DK], BF16)
        sqd_t = SB(f"d_sqd{pr}", [128, DK], F32)
        sqd = sqd_t[:]
        for hh in range(2):
            h = 2 * pr + hh
            for slot, wn in ((qslot, "dwq"), (kslot, "dwk")):
                w_dma(P, "g", None, slot[:], c.inp[wn][h * 128:(h + 1) * 128, :],
                      dstkey=key(slot), srckey=key(c.inp[wn]))
            for which, slot in (("q", qslot), ("k", kslot), ("v", vslot)):
                if which == "v":
                    w_dma(P, "g", None, vslot[:], c.inp["dwv"][h * 128:(h + 1) * 128, :],
                          dstkey=key(vslot), srckey=key(c.inp["dwv"]))
                for t in range(NT):
                    ps = c.ps[t % 4]
                    pk = f"psm{t % 4}"
                    for d in range(ND):
                        w_matmul(P, ps[:, 0:DK], c.xb[d][:, t * 128:(t + 1) * 128],
                                 slot[:, d * DK:(d + 1) * DK], d == 0, d == ND - 1, pk)
                    if which == "v":
                        w_copy(P, "v", vtok[t][:, hh * DK:(hh + 1) * DK], ps[:, 0:DK],
                               dstkey=key(vtok[t]), srckey=pk)
                        continue
                    w_act(P, sqd, ps[:, 0:DK], AF.Square, accum=nsq[:],
                          dkey=key(sqd_t), rkeys=[pk])
                    P.op("v", lambda v, nsq=nsq: v.reciprocal(nsq[:], nsq[:]),
                         reads=[key(nsq)], writes=[key(nsq)])
                    P.op("s", lambda s, nsq=nsq: s.activation(nsq[:], nsq[:], AF.Sqrt),
                         reads=[key(nsq)], writes=[key(nsq)])
                    dst_tok = qn if which == "q" else ktok[t]
                    dsl = dst_tok[:] if which == "q" else dst_tok[:, hh * DK:(hh + 1) * DK]
                    P.op("v", lambda v, dsl=dsl, ps=ps, nsq=nsq, pk=pk: v.tensor_single_scalar(
                        dsl, ps[:, 0:DK], nsq[:], ALU.mult),
                        reads=[pk, key(nsq)], writes=[key(dst_tok)])
                    dstT = QT[hh] if which == "q" else KT[hh]
                    for j in range(2):
                        tb = (2 * t + j) % 3
                        if tb == 0:
                            tps, tkey = c.pst[:, j * 128:j * 128 + 128], "pst"
                        else:
                            tps = c.psx[tb - 1][:].bitcast(BF16)[:, 0:128]
                            tkey = f"psx{tb - 1}"
                        src_ap = (qn[:, j * 128:(j + 1) * 128] if which == "q"
                                  else ktok[t][:, hh * DK + j * 128:hh * DK + (j + 1) * 128])
                        w_transpose(P, tps, src_ap, id_bf[:], tkey)
                        w_copy(P, "s", dstT[j][:, t * 128:(t + 1) * 128], tps,
                               dstkey=key(dstT[j]), srckey=tkey)
        c.pop_scope()
        S32 = [[SB(f"d_s32_{pr}_{hh}_{j}", [128, DK], F32) for j in range(2)] for hh in range(2)]
        Sbf = [[SB(f"d_sbf_{pr}_{hh}_{j}", [128, DK], BF16) for j in range(2)] for hh in range(2)]
        Msb0 = SB(f"d_msb{pr}", [128, 128], F32)
        Msb = [Msb0, Msb0]
        tmpf0 = SB(f"d_tmpf{pr}", [128, 256], F32)
        tmpf = [tmpf0, tmpf0]
        Pbf = [SB(f"d_pbf{pr}_{hh}", [128, 128], BF16) for hh in range(2)]
        PTbf = [SB(f"d_ptbf{pr}_{hh}", [128, 128], BF16) for hh in range(2)]
        TTbf = [SB(f"d_ttbf{pr}_{hh}", [128, 128], BF16) for hh in range(2)]
        Xbf = [SB(f"d_xbf{pr}_{hh}", [128, DK], BF16) for hh in range(2)]
        oto = [SB(f"d_oto{pr}_{hh}", [128, DK], BF16) for hh in range(2)]
        sqd = tmpf0[:, 0:256]


        def prefetch_MK(ci2):
            bank = c.psx[ci2 % 2]
            bkey = f"psx{ci2 % 2}"
            cs2 = slice(ci2 * 128, ci2 * 128 + 128)
            for hh in range(2):
                off = 256 * hh
                for j in range(2):
                    w_matmul(P, bank[:, off:off + 128], KT[hh][j][:, cs2],
                             KT[hh][j][:, cs2], j == 0, j == 1, bkey)
                for j in range(2):
                    w_matmul(P, bank[:, off + 128:off + 256], KT[hh][j][:, cs2],
                             QT[hh][j][:, cs2], j == 0, j == 1, bkey)
        prefetch_MK(0)
        for ci in range(NC):
            cs = slice(ci * 128, ci * 128 + 128)
            first = ci == 0
            mkbank = c.psx[ci % 2]
            mkkey = f"psx{ci % 2}"
            for hh in range(2):
                h = 2 * pr + hh
                bcol = beta[:, ci * 4 + h:ci * 4 + h + 1]
                if not first:
                    for j in range(2):
                        w_copy(P, "v", Sbf[hh][j][:], S32[hh][j][:])
                off = 256 * hh
                w_copy(P, "v", Msb[hh][:], mkbank[:, off:off + 128], srckey=mkkey)
                w_tt(P, "v", ALU.mult, TTbf[hh][:], mkbank[:, off + 128:off + 256], tu_f[:],
                     dkey=key(TTbf[hh]), rkeys=[mkkey, key(tu_f)])
                w_ts(P, "v", tmpf[hh][:, 0:128], Msb[hh][:], bcol, None, ALU.mult,
                     dkey=key(tmpf[hh]), rkeys=[key(Msb[hh]), key(beta)])
                w_tt(P, "v", ALU.mult, Pbf[hh][:], tmpf[hh][:, 0:128], slneg[:],
                     dkey=key(Pbf[hh]), rkeys=[key(tmpf[hh]), key(slneg)])
                w_transpose(P, c.pst[:, (4 + hh) * 128:(5 + hh) * 128], Pbf[hh][:],
                            id_bf[:], "pst")
                w_copy(P, "v", PTbf[hh][:], c.pst[:, (4 + hh) * 128:(5 + hh) * 128],
                       dstkey=key(PTbf[hh]), srckey="pst")
                if first:
                    P.op("v", lambda v, hh=hh, ci=ci, bcol=bcol, Xbf=Xbf, vtok=vtok:
                        v.tensor_single_scalar(
                        Xbf[hh][:], vtok[ci][:, hh * DK:(hh + 1) * DK], bcol, ALU.mult),
                        reads=[key(vtok[ci]), key(beta)], writes=[key(Xbf[hh])])
                else:
                    for j in range(2):
                        w_matmul(P, bankX[hh][:, 0:DK], KT[hh][j][:, cs], Sbf[hh][j][:],
                                 j == 0, j == 1, kXa[hh])
                    P.op("v", lambda v, hh=hh, ci=ci, tmpf=tmpf, vtok=vtok, bankX=bankX:
                        v.tensor_tensor(
                        tmpf[hh][:, 0:256], vtok[ci][:, hh * DK:(hh + 1) * DK],
                        bankX[hh][:, 0:DK], ALU.subtract),
                        reads=[key(vtok[ci]), kXa[hh]], writes=[key(tmpf[hh])])
                    P.op("v", lambda v, hh=hh, bcol=bcol, Xbf=Xbf, tmpf=tmpf:
                        v.tensor_single_scalar(
                        Xbf[hh][:], tmpf[hh][:, 0:256], bcol, ALU.mult),
                        reads=[key(tmpf[hh]), key(beta)], writes=[key(Xbf[hh])])
            for lev in range(7):
                for hh in range(2):
                    w_matmul(P, bankX[hh][:, 0:DK], id_bf[:], Xbf[hh][:], True, False, kXa[hh])
                    w_matmul(P, bankX[hh][:, 0:DK], PTbf[hh][:], Xbf[hh][:], False, True, kXa[hh])
                for hh in range(2):
                    w_copy(P, "v", Xbf[hh][:], bankX[hh][:, 0:DK], srckey=kXa[hh])
                    if lev < 6:
                        w_matmul(P, bankP[hh][:, 0:128], PTbf[hh][:], Pbf[hh][:],
                                 True, True, kPa[hh])
                        w_matmul(P, bankP[hh][:, 256:384], Pbf[hh][:], PTbf[hh][:],
                                 True, True, kPb[hh])
                for hh in range(2):
                    if lev < 6:
                        w_copy(P, "s", Pbf[hh][:], bankP[hh][:, 0:128], srckey=kPa[hh])
                        w_copy(P, "s", PTbf[hh][:], bankP[hh][:, 256:384], srckey=kPb[hh])
                if lev == 2 and ci + 1 < NC:
                    prefetch_MK(ci + 1)
            for hh in range(2):
                for j in range(2):
                    kk = kPa[hh] if j == 0 else kPb[hh]
                    off = 0 if j == 0 else 256
                    w_matmul(P, bankP[hh][:, off:off + DK],
                             ktok[ci][:, hh * DK + j * 128:hh * DK + (j + 1) * 128],
                             Xbf[hh][:], True, True, kk)
                    if first:
                        w_copy(P, "v", S32[hh][j][:], bankP[hh][:, off:off + DK], srckey=kk)
                    else:
                        P.op("v", lambda v, hh=hh, j=j, off=off, S32=S32, bankP=bankP:
                            v.tensor_tensor(
                            S32[hh][j][:], S32[hh][j][:], bankP[hh][:, off:off + DK],
                            ALU.add), reads=[key(S32[hh][j]), kk], writes=[key(S32[hh][j])])
            for hh in range(2):
                h = 2 * pr + hh
                obank = c.psr if hh == 0 else bankP[1]
                obkey = "psr" if hh == 0 else kPa[1]
                osl = obank[:, 0:DK] if hh == 0 else obank[:, 256:512]
                if first:
                    w_matmul(P, osl, TTbf[hh][:], Xbf[hh][:], True, True, obkey)
                else:
                    for j in range(2):
                        w_matmul(P, osl, QT[hh][j][:, cs], Sbf[hh][j][:],
                                 j == 0, False, obkey)
                    w_matmul(P, osl, TTbf[hh][:], Xbf[hh][:], False, True, obkey)
                w_act(P, sqd, osl, AF.Square, accum=nsq[:],
                      dkey=key(tmpf[0]), rkeys=[obkey])
                w_ts(P, "v", nsq[:], nsq[:], 1.0 / DK, 1e-5, ALU.mult, ALU.add,
                     dkey=key(nsq), rkeys=[key(nsq)])
                P.op("v", lambda v, nsq=nsq: v.reciprocal(nsq[:], nsq[:]),
                     reads=[key(nsq)], writes=[key(nsq)])
                P.op("s", lambda s, nsq=nsq: s.activation(nsq[:], nsq[:], AF.Sqrt),
                     reads=[key(nsq)], writes=[key(nsq)])
                w_stt(P, "v", oto[hh][:], osl, nsq[:], gnb[:],
                      ALU.mult, ALU.mult, dkey=key(oto[hh]),
                      rkeys=[obkey, key(nsq), key(gnb)])
                for j in range(2):
                    if j == 0:
                        tps = c.pst[:, (2 * hh) * 128:(2 * hh) * 128 + 128]
                        tkey = "pst"
                    else:
                        tps = c.psx[ci % 2][:].bitcast(BF16)[:, hh * 128:hh * 128 + 128]
                        tkey = f"psx{ci % 2}"
                    w_transpose(P, tps, oto[hh][:, j * 128:(j + 1) * 128], id_bf[:], tkey)
                    w_copy(P, "s", OT[2 * h + j][:, cs], tps,
                           dstkey=key(OT[2 * h + j]), srckey=tkey)
        c.pop_scope()

    wo = SB("d_wo", [128, 8 * D], BF16)
    load_weight_bf16(c, wo, c.inp["dwo"], D, D)
    res = [SB(f"d_res{d}", [128, 512], F32) for d in range(ND)]
    lsc = ln_scratch(c, "d_ln", W=512)
    for ch in range(4):
        for d in range(ND):
            ps = c.ps[d % 2]
            pk = f"psm{d % 2}"
            for n in range(ND):
                w_matmul(P, ps[:, 0:512],
                         wo[:, n * D + d * 128:n * D + d * 128 + 128],
                         OT[n][:, ch * 512:ch * 512 + 512],
                         n == 0, n == ND - 1, pk)
            P.op("v", lambda v, d=d, ps=ps, ch=ch: v.tensor_tensor(
                res[d][:], ps[:, 0:512], c.xt[d][:, ch * 512:ch * 512 + 512], ALU.add),
                reads=[pk, c.xt[d][:, ch * 512:ch * 512 + 512]], writes=[key(res[d])])
        ln_update(c, res, None, lsc, W=512, col0=ch * 512)


L0F_NQ = 4


def emit_l0f(c):
    P = c.P
    SB = c.SB
    E = 8
    id_f = load_const(c, "s_id_f", "c_id_f")
    oner_f = load_const(c, "s_oner_f", "c_oner_f")
    slotA = SB("m_slotA", [128, 8 * D], BF16)
    slotB = SB("m_slotB", [128, 8 * D], BF16)
    zh = [SB(f"m_zh{i}", [128, 512], BF16) for i in range(16)]
    g = [SB(f"m_g{i}", [128, 512], BF16) for i in range(8)]
    xw = [SB(f"m_xw{i}", [128, 512], BF16) for i in range(8)]
    yq = [SB(f"m_y{i}", [128, 512], F32) for i in range(8)]
    gws = SB("m_gws", [128, 64], F32)
    b1c = SB("m_b1c", [128, 64], F32)
    wselq = SB("m_wsel", [128, 32], F32)
    keepq = SB("m_keep", [128, 32], F32)
    pcol = SB("m_pcol", [128, 8], F32)
    m1c = SB("m_m1c", [128, 1], F32)
    rsc = SB("m_rsc", [128, 1], F32)
    wrow = SB("m_wrow", [1, 512], F32)
    b2c = SB("m_b2c", [128, 64], F32)
    b3s = SB("m_b3s", [1, D], BF16)
    mrow_bf = SB("m_mrowbf", [1, 512], BF16)
    x3m = [SB(f"m_x3m{d}", [128, 512], BF16) for d in range(ND)]
    prod = SB("m_prod", [128, 512], F32)
    lsc = ln_scratch(c, "m_ln", W=512)
    w_dma(P, "y", None, gws[:], c.inp["gwT"])
    w_dma(P, "y", None, b1c[:], c.inp["b1c"])
    w_dma(P, "y", None, b2c[:], c.inp["b2c"])

    w1T, w2T, w3T = c.inp["w1T"], c.inp["w2T"], c.inp["w3T"]
    sw1T, sw2T, sw3T = c.inp["sw1T"], c.inp["sw2T"], c.inp["sw3T"]

    def load_slot(slot, dram, blk):
        w_dma(P, "g", None, slot[:], dram[blk * 128:(blk + 1) * 128, :],
              dstkey=key(slot), srckey=key(dram))

    for tq in range(L0F_NQ):
        t0 = tq * 512
        for ih in range(2):
            load_slot(slotA, sw1T, ih)
            load_slot(slotB, sw2T, ih)
            for i8 in range(8):
                i = ih * 8 + i8
                ps = c.ps[i8 % 2]
                pk = f"psm{i8 % 2}"
                for d in range(ND):
                    w_matmul(P, ps[:, 0:512],
                             slotA[:, d * 1024 + i8 * 128:d * 1024 + i8 * 128 + 128],
                             c.xb[d][:, t0:t0 + 512], d == 0, d == ND - 1, pk)
                w_act(P, g[i8][:], ps[:, 0:512], AF.Silu, dkey=key(g[i8]), rkeys=[pk])
            for i8 in range(8):
                i = ih * 8 + i8
                ps = c.ps[i8 % 2]
                pk = f"psm{i8 % 2}"
                for d in range(ND):
                    w_matmul(P, ps[:, 0:512],
                             slotB[:, d * 1024 + i8 * 128:d * 1024 + i8 * 128 + 128],
                             c.xb[d][:, t0:t0 + 512], d == 0, d == ND - 1, pk)
                w_stt(P, "v", zh[i][:], ps[:, 0:512], 1.0, g[i8][:], ALU.mult, ALU.mult,
                      dkey=key(zh[i]), rkeys=[pk, key(g[i8])])
            if ih == 0:
                load_slot(slotA, sw1T, 1)
        load_slot(slotA, sw3T, 0)
        load_slot(slotB, sw3T, 1)
        for d in range(ND):
            ps = c.ps[d % 2]
            pk = f"psm{d % 2}"
            for i in range(8):
                w_matmul(P, ps[:, 0:512],
                         slotA[:, i * 1024 + d * 128:i * 1024 + d * 128 + 128],
                         zh[i][:], i == 0, False, pk)
            for i in range(8):
                w_matmul(P, ps[:, 0:512],
                         slotB[:, i * 1024 + d * 128:i * 1024 + d * 128 + 128],
                         zh[8 + i][:], False, i == 7, pk)
            w_copy(P, "v" if d % 2 == 0 else "s", yq[d][:], ps[:, 0:512], srckey=pk)

        for tt in range(4):
            ti = tq * 4 + tt
            pr = c.psr
            for d in range(ND):
                w_matmul(P, pr[:, 0:8], c.xt[d][:, ti * 128:(ti + 1) * 128],
                         gws[:, d * 8:(d + 1) * 8], d == 0, d == ND - 1, "psr")
            w_reduce(P, "v", m1c[:], pr[:, 0:8], ALU.max, srckey="psr", negate=True)
            w_act(P, pcol[:], pr[:, 0:8], AF.Exp, bias=m1c[:], accum=rsc[:],
                  dkey=key(pcol), rkeys=["psr", key(m1c)])
            P.op("v", lambda v: v.reciprocal(rsc[:], rsc[:]),
                 reads=[key(rsc)], writes=[key(rsc)])
            w_ts(P, "v", pcol[:], pcol[:], rsc[:], None, ALU.mult,
                 dkey=key(pcol), rkeys=[key(pcol), key(rsc)])
            w_reduce(P, "v", m1c[:], pcol[:], ALU.max, srckey=key(pcol), dkey=key(m1c))
            w_ts(P, "v", keepq[:, tt * 8:(tt + 1) * 8], pcol[:], m1c[:], None,
                 ALU.is_ge, dkey=key(keepq), rkeys=[key(pcol), key(m1c)])
            w_stt(P, "v", wselq[:, tt * 8:(tt + 1) * 8], keepq[:, tt * 8:(tt + 1) * 8],
                  -1e30, pcol[:], ALU.mult, ALU.add,
                  dkey=key(wselq), rkeys=[key(keepq), key(pcol)])
            w_reduce(P, "v", m1c[:], wselq[:, tt * 8:(tt + 1) * 8], ALU.max,
                     srckey=key(wselq), dkey=key(m1c))
            w_ts(P, "v", keepq[:, tt * 8:(tt + 1) * 8], pcol[:], m1c[:], None,
                 ALU.is_ge, dkey=key(keepq), rkeys=[key(pcol), key(m1c)])
            w_tt(P, "v", ALU.mult, wselq[:, tt * 8:(tt + 1) * 8], pcol[:],
                 keepq[:, tt * 8:(tt + 1) * 8],
                 dkey=key(wselq), rkeys=[key(pcol), key(keepq)])

        load_slot(slotA, w1T, 0)
        load_slot(slotB, w3T, 0)
        for e in range(E):
            sl1, sl3 = slotA, slotB
            for tt in range(4):
                w_matmul(P, c.psr[0:1, 0:128], wselq[:, tt * 8 + e:tt * 8 + e + 1],
                         id_f[:], True, True, "psr")
                w_copy(P, "v", wrow[0:1, tt * 128:(tt + 1) * 128], c.psr[0:1, 0:128],
                       dstkey=key(wrow), srckey="psr")
                w_matmul(P, c.psr[0:1, 0:128], keepq[:, tt * 8 + e:tt * 8 + e + 1],
                         id_f[:], True, True, "psr")
                w_copy(P, "v", mrow_bf[0:1, tt * 128:(tt + 1) * 128], c.psr[0:1, 0:128],
                       dstkey=key(mrow_bf), srckey="psr")
            w_dma(P, "g", None, b3s[:], c.inp["b3r"][e:e + 1, :])
            w_matmul(P, c.psx[0][:, 0:512], c.one1_f[:], wrow[0:1, 0:512], True, True, "psx0")
            for d in range(ND):
                w_tt(P, "v", ALU.mult, xw[d][:], c.xt[d][:, t0:t0 + 512],
                     c.psx[0][:, 0:512], dkey=key(xw[d]),
                     rkeys=[c.xt[d][:, t0:t0 + 512], "psx0"])
            for d in range(ND):
                ps3 = c.ps[2 + d % 2]
                pk3 = f"psm{2 + d % 2}"
                for dd in range(ND):
                    w_matmul(P, ps3[:, 0:512],
                             sl3[:, dd * 1024 + d * 128:dd * 1024 + d * 128 + 128],
                             xw[dd][:], dd == 0, False, pk3)
                w_matmul(P, ps3[:, 0:512], b3s[0:1, d * 128:(d + 1) * 128],
                         mrow_bf[0:1, 0:512], False, True, pk3)
                w_copy(P, "s", x3m[d][:], ps3[:, 0:512], srckey=pk3)
            load_slot(sl3, w2T, e)
            for h in range(ND):
                ps = c.ps[h % 2]
                pk = f"psm{h % 2}"
                for d in range(ND):
                    w_matmul(P, ps[:, 0:512],
                             sl1[:, d * 1024 + h * 128:d * 1024 + h * 128 + 128],
                             xw[d][:], d == 0, d == ND - 1, pk)
                w_act(P, g[h][:], ps[:, 0:512], AF.Silu, bias=b1c[:, e * 8 + h:e * 8 + h + 1],
                      dkey=key(g[h]), rkeys=[pk, key(b1c)])
            if e + 1 < E:
                load_slot(sl1, w1T, e + 1)
            for d in range(ND):
                ps2 = c.ps[d % 2]
                pk2 = f"psm{d % 2}"
                for h in range(ND):
                    w_matmul(P, ps2[:, 0:512],
                             sl3[:, h * 1024 + d * 128:h * 1024 + d * 128 + 128],
                             g[h][:], h == 0, h == ND - 1, pk2)
                w_stt(P, "v", prod[:], ps2[:, 0:512], b2c[:, e * 8 + d:e * 8 + d + 1],
                      x3m[d][:], ALU.add, ALU.mult,
                      dkey=key(prod), rkeys=[pk2, key(b2c), key(x3m[d])])
                P.op("v", lambda v, d=d: v.tensor_tensor(yq[d][:], yq[d][:], prod[:],
                     ALU.add), reads=[key(yq[d]), key(prod)], writes=[key(yq[d])])
            if e + 1 < E:
                load_slot(slotB, w3T, e + 1)

        for d in range(ND):
            P.op("g", lambda gp, d=d, t0=t0: gp.tensor_tensor(yq[d][:], yq[d][:],
                 c.xt[d][:, t0:t0 + 512], ALU.add),
                 reads=[key(yq[d]), c.xt[d][:, t0:t0 + 512]], writes=[key(yq[d])])
        ln_update(c, [y[:] for y in yq], None, lsc, W=512, col0=t0)


STAGE_INPUTS = {
    "l1f": [("wrT", [128, 8 * D], F32), ("wkT", [128, 8 * D], F32), ("wvT", [128, 8 * D], F32),
            ("mur", [128, 8], F32), ("muk", [128, 8], F32)],
    "l1a": [("aqT", [128, 8 * D], F32), ("akT", [128, 8 * D], F32), ("avT", [128, 8 * D], F32),
            ("aoT", [128, 8 * D], F32)],
    "l0a": [("dwq", [512, 8 * 256], F32), ("dwk", [512, 8 * 256], F32), ("dwv", [512, 8 * 256], F32),
            ("dwbeta", [128, 32], F32), ("dwo", [128, 8 * D], F32), ("dgn", [1, 256], F32)],
    "l0f": [("gwT", [128, 64], F32), ("b1c", [128, 64], F32), ("b2c", [128, 64], F32),
            ("b3r", [8, D], F32),
            ("w1T", [8 * 128, 8 * D], F32), ("w2T", [8 * 128, 8 * D], F32),
            ("w3T", [8 * 128, 8 * D], F32),
            ("sw1T", [2 * 128, 8 * D], F32), ("sw2T", [2 * 128, 8 * D], F32),
            ("sw3T", [2 * 128, 8 * D], F32)],
}
STAGE_EMIT = {"l1f": emit_l1f, "l1a": emit_l1a, "l0f": emit_l0f, "l0a": emit_l0a}


def build_program(stages):
    nc = bass.Bass("TRN2", target_bir_lowering=False)
    es = ExitStack()
    need = [("xin", [D, NTOK], F32)] + const_inputs_spec()
    for st in stages:
        need += STAGE_INPUTS[st]
    with es:
        P = Prog(nc, es)
        c = make_ctx(nc, es, P, need)
        blk = es.enter_context(nc.Block())
        emit_load_stream(c)
        for st in stages:
            c.push_scope()
            STAGE_EMIT[st](c)
            c.pop_scope()
        emit_store_stream(c)
        P.finalize(blk)
    return nc


def tile128(a):
    a = np.ascontiguousarray(a)
    R, C = a.shape
    return np.ascontiguousarray(a.reshape(R // 128, 128, C).transpose(1, 0, 2).reshape(128, -1))


def host_inputs_for(stages, inputs):
    x = np.ascontiguousarray(inputs["L0_attn_x"])
    consts = const_inputs_host()
    maps = []
    stage_w = {}
    if "l0a" in stages:
        stage_w.update({
            "dwq": np.concatenate([tile128(np.asarray(inputs["L0_attn_Wq"])[:, h * 256:(h + 1) * 256]) for h in range(4)], axis=0).reshape(512, 8 * 256),
            "dwk": np.concatenate([tile128(np.asarray(inputs["L0_attn_Wk"])[:, h * 256:(h + 1) * 256]) for h in range(4)], axis=0).reshape(512, 8 * 256),
            "dwv": np.concatenate([tile128(np.asarray(inputs["L0_attn_Wv"])[:, h * 256:(h + 1) * 256]) for h in range(4)], axis=0).reshape(512, 8 * 256),
            "dwbeta": np.ascontiguousarray(np.asarray(inputs["L0_attn_Wbeta"]).reshape(8, 128, 4).transpose(1, 0, 2).reshape(128, 32)),
            "dwo": tile128(np.ascontiguousarray(inputs["L0_attn_Wo"])),
            "dgn": np.ascontiguousarray(np.asarray(inputs["L0_attn_g_norm"]).reshape(1, 256)),
        })
    if "l0f" in stages:
        stage_w.update({
            "gwT": np.ascontiguousarray(np.asarray(inputs["L0_ffn_gate_w"]).T.reshape(8, 128, 8).transpose(1, 0, 2).reshape(128, 64)),
            "b1c": np.concatenate([np.ascontiguousarray(
                np.asarray(inputs["L0_ffn_b1"][e]).reshape(8, 128).T) for e in range(8)], axis=1),
            "b2c": np.concatenate([np.ascontiguousarray(
                np.asarray(inputs["L0_ffn_b2"][e]).reshape(8, 128).T) for e in range(8)], axis=1),
            "b3r": np.ascontiguousarray(inputs["L0_ffn_b3"]),
            "w1T": np.concatenate([tile128(np.asarray(inputs["L0_ffn_w1"][e]).T)
                                   for e in range(8)], axis=0),
            "w2T": np.concatenate([tile128(np.asarray(inputs["L0_ffn_w2"][e]).T)
                                   for e in range(8)], axis=0),
            "w3T": np.concatenate([tile128(np.asarray(inputs["L0_ffn_w3"][e]).T)
                                   for e in range(8)], axis=0),
            "sw1T": np.concatenate([tile128(np.asarray(inputs["L0_ffn_sw1"]).T[:, i * 1024:(i + 1) * 1024])
                                    for i in range(2)], axis=0).reshape(256, 8 * D),
            "sw2T": np.concatenate([tile128(np.asarray(inputs["L0_ffn_sw2"]).T[:, i * 1024:(i + 1) * 1024])
                                    for i in range(2)], axis=0).reshape(256, 8 * D),
            "sw3T": np.concatenate([tile128(np.asarray(inputs["L0_ffn_sw3"]).T[i * 1024:(i + 1) * 1024, :])
                                    for i in range(2)], axis=0).reshape(256, 8 * D),
        })
    if "l1a" in stages:
        stage_w.update({
            "aqT": tile128(np.asarray(inputs["L1_attn_Wq"]).reshape(1024, 1024)),
            "akT": tile128(np.asarray(inputs["L1_attn_Wk"]).reshape(1024, 1024)),
            "avT": tile128(np.asarray(inputs["L1_attn_Wv"]).reshape(1024, 1024)),
            "aoT": tile128(np.asarray(inputs["L1_attn_Wout"]).reshape(1024, 1024)),
        })
    if "l1f" in stages:
        stage_w.update({
            "wrT": tile128(np.asarray(inputs["L1_ffn_W_r"]).T),
            "wkT": tile128(np.asarray(inputs["L1_ffn_W_k"]).T),
            "wvT": tile128(np.asarray(inputs["L1_ffn_W_v"]).T),
            "mur": np.ascontiguousarray(inputs["L1_ffn_mu_r"].reshape(8, 128).T),
            "muk": np.ascontiguousarray(inputs["L1_ffn_mu_k"].reshape(8, 128).T),
        })
    for b in range(N_CORES):
        m = {"xin": np.ascontiguousarray(x[b].T)}
        m.update(consts)
        m.update(stage_w)
        maps.append(m)
    return maps


def kernel(**inputs):
    stages = ["l0a", "l0f", "l1a", "l1f"]
    nc = build_program(stages)
    maps = host_inputs_for(stages, inputs)
    res = run_bass_kernel_spmd(nc, maps, core_ids=list(range(N_CORES)))
    out = np.stack([np.ascontiguousarray(r["out"].T) for r in res.results], axis=0)
    return out.astype(np.float32)


# version 25 (final)
# speedup vs baseline: 1.0034x; 1.0034x over previous
import numpy as np
from contextlib import ExitStack

import concourse.bass as bass
import concourse.mybir as mybir
from concourse.bass_utils import run_bass_kernel_spmd

F32 = mybir.dt.float32
BF16 = mybir.dt.bfloat16
AF = mybir.ActivationFunctionType
ALU = mybir.AluOpType
AX = mybir.AxisListType

N_CORES = 8
NTOK = 2048
D = 1024
NT = NTOK // 128
ND = D // 128


class Prog:
    CHAINS = ("t", "v", "s", "g", "y")

    def __init__(self, nc, es):
        self.nc = nc
        self.es = es
        self.ops = {e: [] for e in self.CHAINS}
        self.cnt = {e: 0 for e in self.CHAINS}
        self.sems = {}
        for e in self.CHAINS:
            self.sems["ch_" + e] = es.enter_context(nc.semaphore("ch_" + e))
        self.dma_issued = {}
        self.last_write = {}
        self.readers = {}
        self.clock = {e: {} for e in self.CHAINS}
        self.snap = {}

    def dma_sem(self, name):
        if name not in self.dma_issued:
            self.sems[name] = self.es.enter_context(self.nc.semaphore(name))
            self.dma_issued[name] = 0
        return name

    def _waits_for(self, e, reads, writes):
        toks = []
        for k in reads:
            t = self.last_write.get(k)
            if t:
                toks.append(t)
        for k in writes:
            t = self.last_write.get(k)
            if t:
                toks.append(t)
            toks.extend(self.readers.get(k, ()))
        need = {}
        clk = self.clock[e]
        for (sn, v) in toks:
            if sn in self.dma_issued:
                v = 16 * self.dma_issued[sn]
            if e == "t" and sn == "ch_t":
                continue
            if clk.get(sn, -1) < v:
                need[sn] = max(need.get(sn, 0), v)
                s2 = self.snap.get((sn, v))
                if s2:
                    for sn2, v2 in s2.items():
                        if clk.get(sn2, -1) < v2:
                            clk[sn2] = v2
        return need

    def _record(self, e, reads, writes, token):
        for k in writes:
            self.last_write[k] = token
            self.readers[k] = []
        for k in reads:
            self.readers.setdefault(k, []).append(token)

    def op(self, e, fn, reads=(), writes=(), skip_dep_check=False, inc=1):
        reads = norm_keys(reads)
        writes = norm_keys(writes)
        need = {} if skip_dep_check else self._waits_for(e, reads, writes)
        self.cnt[e] += 1
        myval = self.cnt[e]
        mysem = "ch_" + e
        sems = self.sems

        def emit(eng, need=need, fn=fn, mysem=mysem):
            for sn in sorted(need):
                eng.wait_ge(sems[sn], need[sn])
            ins = fn(eng)
            ins.then_inc(sems[mysem], 1)
        self.ops[e].append(emit)
        for sn, v in need.items():
            self.clock[e][sn] = max(self.clock[e].get(sn, -1), v)
        token = (mysem, myval)
        snap = dict(self.clock[e])
        snap[mysem] = myval - 1
        self.snap[token] = snap
        self._record(e, reads, writes, token)
        return token

    def dma(self, e, semname, fn, reads=(), writes=()):
        self.dma_sem(semname)
        reads = norm_keys(reads)
        writes = norm_keys(writes)
        need = self._waits_for(e, reads, writes)
        self.dma_issued[semname] += 1
        myval = 16 * self.dma_issued[semname]
        sems = self.sems

        def emit(eng, need=need, fn=fn, semname=semname):
            for sn in sorted(need):
                eng.wait_ge(sems[sn], need[sn])
            fn(eng).then_inc(sems[semname], 16)
        self.ops[e].append(emit)
        for sn, v in need.items():
            self.clock[e][sn] = max(self.clock[e].get(sn, -1), v)
        token = (semname, myval)
        self.snap[token] = dict(self.clock[e])
        self._record(e, reads, writes, token)
        return token

    def dma_batch(self, e, semname, fns, reads=(), writes=()):
        self.dma_sem(semname)
        reads = norm_keys(reads)
        writes = norm_keys(writes)
        need = self._waits_for(e, reads, writes)
        sems = self.sems

        def emit(eng, need=need, fns=fns, semname=semname):
            for sn in sorted(need):
                eng.wait_ge(sems[sn], need[sn])
            for fn in fns:
                fn(eng).then_inc(sems[semname], 16)
        self.ops[e].append(emit)
        for sn, v in need.items():
            self.clock[e][sn] = max(self.clock[e].get(sn, -1), v)
        self.dma_issued[semname] += len(fns)
        token = (semname, 16 * self.dma_issued[semname])
        self.snap[token] = dict(self.clock[e])
        self._record(e, reads, writes, token)
        return token

    def full_barrier(self):
        snap = [("ch_" + e, self.cnt[e]) for e in self.CHAINS]
        snap += [(s, 16 * n) for s, n in self.dma_issued.items()]
        for e in self.CHAINS:
            for sn, v in snap:
                if sn == "ch_" + e or v <= 0:
                    continue
                if self.clock[e].get(sn, -1) < v:
                    self.barrier_wait(e, sn, v)
                    self.clock[e][sn] = v

    def barrier_wait(self, e, semname, val):
        sems = self.sems

        def emit(eng):
            eng.wait_ge(sems[semname], val)
        self.ops[e].append(emit)

    def finalize(self, blk):
        prog = self

        def mk(e):
            def run(eng):
                for f in prog.ops[e]:
                    f(eng)
            return run
        blk.tensor(mk("t"))
        blk.vector(mk("v"))
        blk.scalar(mk("s"))
        blk.gpsimd(mk("g"))
        blk.sync(mk("y"))


def key(t, tag=""):
    return t.tensor.name if isinstance(t, bass.AP) else t.name


STREAM_RG = 512


def ap_keys(ap):
    if not isinstance(ap, bass.AP):
        return [ap]
    name = ap.tensor.name
    if (name.startswith("xt") or name.startswith("xb")) and name[2:].isdigit():
        rs = int(ap.tensor.shape[-1])
        off = int(ap.offset) % rs
        inner = ap.ap[-1]
        assert int(inner[0]) == 1
        lo, hi = off, off + int(inner[1])
        return [f"{name}@r{r}" for r in range(lo // STREAM_RG, (hi + STREAM_RG - 1) // STREAM_RG)]
    return [name]


def norm_keys(lst):
    out = []
    for x in lst:
        out.extend(ap_keys(x))
    return out


def w_matmul(P, out_ps, lhsT, rhs, start, stop, pskey, extra_reads=()):
    reads = [lhsT, rhs] + list(extra_reads)
    if start:
        writes = [pskey]
        tok = P.op("t", lambda t: t.matmul(out_ps, lhsT, rhs, start=start, stop=stop),
                   reads=reads, writes=writes)
    else:
        tok = P.op("t", lambda t: t.matmul(out_ps, lhsT, rhs, start=start, stop=stop),
                   reads=reads, writes=[])
        if stop:
            P.last_write[pskey] = tok
            P.readers[pskey] = []
    return tok


def w_transpose(P, out_ps, in_sb, ident, pskey):
    return P.op("t", lambda t: t.transpose(out_ps, in_sb, ident),
                reads=[in_sb, ident], writes=[pskey])


def w_copy(P, e, dst, src, dstkey=None, srckey=None):
    if e == "s":
        return P.op(e, lambda x: x.copy(dst, src),
                    reads=[srckey if srckey is not None else src],
                    writes=[dstkey if dstkey is not None else dst])
    return P.op(e, lambda x: x.tensor_copy(dst, src),
                reads=[srckey if srckey is not None else src],
                writes=[dstkey if dstkey is not None else dst])


def w_tt(P, e, op, dst, a, b, dkey=None, rkeys=None):
    return P.op(e, lambda x: x.tensor_tensor(dst, a, b, op),
                reads=rkeys if rkeys is not None else [a, b],
                writes=[dkey if dkey is not None else dst])


def w_ts(P, e, dst, a, s1, s2, op0, op1=ALU.bypass, dkey=None, rkeys=None):
    if s2 is None:
        return P.op(e, lambda x: x.tensor_single_scalar(dst, a, s1, op0),
                    reads=rkeys if rkeys is not None else [a] + ([s1] if isinstance(s1, bass.AP) else []),
                    writes=[dkey if dkey is not None else dst])
    return P.op(e, lambda x: x.tensor_scalar(dst, a, s1, s2, op0, op1),
                reads=rkeys if rkeys is not None else [a] + [s for s in (s1, s2) if isinstance(s, bass.AP)],
                writes=[dkey if dkey is not None else dst])


def w_stt(P, e, dst, a, scalar, b, op0, op1, dkey=None, rkeys=None):
    return P.op(e, lambda x: x.scalar_tensor_tensor(dst, a, scalar, b, op0, op1),
                reads=rkeys if rkeys is not None else [a, b] + ([scalar] if isinstance(scalar, bass.AP) else []),
                writes=[dkey if dkey is not None else dst])


def w_act(P, dst, src, func, bias=0.0, scale=1.0, accum=None, dkey=None, rkeys=None):
    writes = [dkey if dkey is not None else dst] + ([accum] if accum is not None else [])
    reads = rkeys if rkeys is not None else [src] + [x for x in (bias, scale) if isinstance(x, bass.AP)]
    return P.op("s", lambda s: s.activation(dst, src, func, bias=bias, scale=scale,
                                            accum_out=accum),
                reads=reads, writes=writes)


def w_reduce(P, e, dst, src, op, dkey=None, srckey=None, negate=None):
    return P.op(e, lambda x: x.tensor_reduce(dst, src, AX.X, op, negate=negate),
                reads=[srckey if srckey is not None else src],
                writes=[dkey if dkey is not None else dst])


def w_memset(P, e, dst, val, dkey=None):
    return P.op(e, lambda x: x.memset(dst, val),
                reads=[], writes=[dkey if dkey is not None else dst])


def w_recip(P, dst, src, dkey=None, srckey=None):
    return P.op("v", lambda v: v.reciprocal(dst, src),
                reads=[srckey if srckey is not None else src],
                writes=[dkey if dkey is not None else dst])


def w_dma(P, e, semname, dst, src, dstkey=None, srckey=None):
    if semname is None:
        semname = "dm_" + key(dst)
    return P.dma(e, semname, lambda x: x.dma_start(out=dst, in_=src),
                 reads=[srckey if srckey is not None else src],
                 writes=[dstkey if dstkey is not None else dst])


def w_dma_batch(P, e, pairs, dstkey, srckeys, semname=None):
    if semname is None:
        semname = "dm_" + dstkey
    fns = [(lambda x, d=d, s=s: x.dma_start(out=d, in_=s)) for d, s in pairs]
    return P.dma_batch(e, semname, fns, reads=srckeys, writes=[dstkey])


class Ctx:
    pass


def make_ctx(nc, es, P, need_inputs):
    c = Ctx()
    c.nc, c.es, c.P = nc, es, P
    c.inp = {}
    for nm, shp, dt in need_inputs:
        c.inp[nm] = nc.dram_tensor(nm, shp, dt, kind="ExternalInput").ap()
    c.out = nc.dram_tensor("out", [D, NTOK], F32, kind="ExternalOutput").ap()

    c._scopes = [es]

    def SB(n, s, d):
        return c._scopes[-1].enter_context(nc.sbuf_tensor(n, s, d))

    def PS(n, s, d):
        return c._scopes[-1].enter_context(nc.psum_tensor(n, s, d))

    def push_scope():
        c._scopes.append(ExitStack())

    def pop_scope():
        c.P.full_barrier()
        c._scopes.pop().close()
    c.SB, c.PS = SB, PS
    c.push_scope, c.pop_scope = push_scope, pop_scope
    c.xt = [SB(f"xt{i}", [128, NTOK], F32) for i in range(ND)]
    c.xb = [SB(f"xb{i}", [128, NTOK], BF16) for i in range(ND)]
    c.onec_f = SB("s_c_onec_f", [128, 1], F32)
    c.onec_bf = SB("s_c_onec_bf", [128, 1], BF16)
    c.one1_f = SB("s_c_one1_f", [1, 128], F32)
    c.one1_bf = SB("s_c_one1_bf", [1, 128], BF16)
    c.ps = [PS(f"psm{i}", [128, 512], F32) for i in range(4)]
    c.psx = [PS(f"psx{i}", [128, 512], F32) for i in range(2)]
    c.pst = PS("pst", [128, 1024], BF16)
    c.psr = PS("psr", [128, 512], F32)
    for t, nm in [(c.onec_f, "c_onec_f"), (c.one1_f, "c_one1_f"),
                  (c.one1_bf, "c_one1_bf"), (c.onec_bf, "c_onec_bf")]:
        w_dma(P, "y", None, t[:], c.inp[nm])
    return c


def load_const(c, sbuf_name, inp_name):
    ap = c.inp[inp_name]
    t = c.SB(sbuf_name, list(ap.shape), ap.dtype)
    w_dma(c.P, "y", None, t[:], ap)
    return t


def const_inputs_spec():
    return [("c_id_bf", [128, 128], BF16), ("c_id_f", [128, 128], F32),
            ("c_onec_f", [128, 1], F32), ("c_onec_bf", [128, 1], BF16),
            ("c_oner_f", [1, 512], F32),
            ("c_one1_f", [1, 128], F32), ("c_one1_bf", [1, 128], BF16),
            ("c_sl_f", [128, 128], F32), ("c_tu_f", [128, 128], F32),
            ("c_slneg", [128, 128], F32), ("c_suneg", [128, 128], F32),
            ("c_cm01", [128, 896], BF16)]


def const_inputs_host():
    import ml_dtypes
    BF = ml_dtypes.bfloat16
    idm = np.eye(128, dtype=np.float32)
    sl = np.tril(np.ones((128, 128), np.float32), -1)
    tu = np.triu(np.ones((128, 128), np.float32), 0)
    f = np.arange(896)[None, :]
    p = np.arange(128)[:, None]
    cm = (f >= p + 384).astype(np.float32)
    slneg = -np.tril(np.ones((128, 128), np.float32), -1)
    suneg = -np.triu(np.ones((128, 128), np.float32), 1)
    return {"c_id_bf": idm.astype(BF), "c_id_f": idm,
            "c_slneg": slneg, "c_suneg": suneg,
            "c_onec_f": np.ones((128, 1), np.float32),
            "c_onec_bf": np.ones((128, 1)).astype(BF),
            "c_oner_f": np.ones((1, 512), np.float32),
            "c_one1_f": np.ones((1, 128), np.float32),
            "c_one1_bf": np.ones((1, 128)).astype(BF),
            "c_sl_f": sl, "c_tu_f": tu, "c_cm01": cm.astype(BF)}


def emit_load_stream(c):
    P = c.P
    for i in range(ND):
        w_dma(P, "y", None, c.xt[i][:], c.inp["xin"][i * 128:(i + 1) * 128, :])
        w_copy(P, "v" if i % 2 == 0 else "s", c.xb[i][:], c.xt[i][:])


def emit_store_stream(c):
    P = c.P
    for i in range(ND):
        w_dma(P, "y", "so", c.out[i * 128:(i + 1) * 128, :], c.xt[i][:])
    P.barrier_wait("y", "so", 16 * P.dma_issued["so"])


def load_weight_bf16(c, dst_sb, dram_ap, rows, cols):
    w_dma(c.P, "g", None, dst_sb[:, 0:(rows // 128) * cols], dram_ap,
          dstkey=key(dst_sb), srckey=key(dram_ap))


def ln_update(c, res, chunk, scr, W=512, col0=None):
    P = c.P
    if col0 is None:
        col0 = chunk * W
    cs = slice(col0, col0 + W)
    rap = [r if isinstance(r, bass.AP) else r[:] for r in res]
    for d in range(ND):
        w_matmul(P, c.psr[0:1, 0:W], c.onec_f[:], rap[d], d == 0, d == ND - 1, "psr")
    w_copy(P, "v", scr["rowm"][:, 0:W], c.psr[0:1, 0:W], dstkey=key(scr["rowm"]), srckey="psr")
    for d in range(ND):
        w_act(P, scr["sq"][:, 0:W], rap[d], AF.Square, dkey=key(scr["sq"]), rkeys=[key(rap[d])])
        w_matmul(P, c.psr[0:1, 0:W], c.onec_bf[:], scr["sq"][:, 0:W], d == 0, d == ND - 1, "psr")
    w_copy(P, "v", scr["rowq"][:, 0:W], c.psr[0:1, 0:W], dstkey=key(scr["rowq"]), srckey="psr")
    w_ts(P, "v", scr["rowm"][:, 0:W], scr["rowm"][:, 0:W], 1.0 / D, None, ALU.mult,
         dkey=key(scr["rowm"]), rkeys=[key(scr["rowm"])])
    w_ts(P, "v", scr["rowq"][:, 0:W], scr["rowq"][:, 0:W], 1.0 / D, None, ALU.mult,
         dkey=key(scr["rowq"]), rkeys=[key(scr["rowq"])])
    w_tt(P, "v", ALU.mult, scr["rowa"][:, 0:W], scr["rowm"][:, 0:W], scr["rowm"][:, 0:W],
         dkey=key(scr["rowa"]), rkeys=[key(scr["rowm"])])
    w_tt(P, "v", ALU.subtract, scr["rowq"][:, 0:W], scr["rowq"][:, 0:W], scr["rowa"][:, 0:W],
         dkey=key(scr["rowq"]), rkeys=[key(scr["rowq"]), key(scr["rowa"])])
    w_ts(P, "v", scr["rowq"][:, 0:W], scr["rowq"][:, 0:W], 1e-5, None, ALU.add,
         dkey=key(scr["rowq"]), rkeys=[key(scr["rowq"])])
    P.op("s", lambda s: s.activation(scr["rowa"][:, 0:W], scr["rowq"][:, 0:W], AF.Sqrt),
         reads=[key(scr["rowq"])], writes=[key(scr["rowa"])])
    P.op("v", lambda v: v.reciprocal(scr["rowq"][:, 0:W], scr["rowa"][:, 0:W]),
         reads=[key(scr["rowa"])], writes=[key(scr["rowq"])])
    w_tt(P, "v", ALU.mult, scr["rowa"][:, 0:W], scr["rowm"][:, 0:W], scr["rowq"][:, 0:W],
         dkey=key(scr["rowa"]), rkeys=[key(scr["rowm"]), key(scr["rowq"])])
    w_ts(P, "v", scr["rowa"][:, 0:W], scr["rowa"][:, 0:W], -1.0, None, ALU.mult,
         dkey=key(scr["rowa"]), rkeys=[key(scr["rowa"])])
    w_matmul(P, c.psx[0][:, 0:W], c.one1_f[:], scr["rowq"][:, 0:W], True, True, "psx0")
    w_matmul(P, c.psx[1][:, 0:W], c.one1_f[:], scr["rowa"][:, 0:W], True, True, "psx1")
    for d in range(ND):
        w_tt(P, "v", ALU.mult, scr["t1"][:, 0:W], rap[d], c.psx[0][:, 0:W],
             dkey=key(scr["t1"]), rkeys=[key(rap[d]), "psx0"])
        P.op("v", lambda v, d=d: v.tensor_tensor(c.xt[d][:, cs], scr["t1"][:, 0:W],
             c.psx[1][:, 0:W], ALU.add), reads=[key(scr["t1"]), "psx1"],
             writes=[c.xt[d][:, cs]])
        P.op("g", lambda g, d=d: g.tensor_copy(c.xb[d][:, cs], c.xt[d][:, cs]),
             reads=[c.xt[d][:, cs]], writes=[c.xb[d][:, cs]])


def ln_scratch(c, pfx, t1=None, W=512):
    return {"sq": c.SB(pfx + "_sq", [128, W], BF16),
            "rowm": c.SB(pfx + "_rowm", [1, W], F32),
            "rowq": c.SB(pfx + "_rowq", [1, W], F32),
            "rowa": c.SB(pfx + "_rowa", [1, W], F32),
            "t1": t1 if t1 is not None else c.SB(pfx + "_t1", [128, W], F32)}


def emit_l1f(c):
    P = c.P
    SB = lambda n, s, d: c.es.enter_context(c.nc.sbuf_tensor(n, s, d))
    wr = SB("f_wr", [128, 8 * D], BF16)
    wk = SB("f_wk", [128, 8 * D], BF16)
    wv = SB("f_wv", [128, 8 * D], BF16)
    mur = SB("f_mur", [128, 8], F32)
    muk = SB("f_muk", [128, 8], F32)
    load_weight_bf16(c, wr, c.inp["wrT"], D, D)
    load_weight_bf16(c, wk, c.inp["wkT"], D, D)
    load_weight_bf16(c, wv, c.inp["wvT"], D, D)
    w_dma(P, "y", None, mur[:], c.inp["mur"])
    w_dma(P, "y", None, muk[:], c.inp["muk"])

    QT = NTOK // 512
    diff = SB("f_diff", [128, 512], F32)
    xr = [SB(f"f_xr{d}", [128, 512], BF16) for d in range(ND)]
    xk = [SB(f"f_xk{d}", [128, 512], BF16) for d in range(ND)]
    rbuf = [SB(f"f_r{d}", [128, 512], BF16) for d in range(ND)]
    krel = [SB("f_kr", [128, 512], BF16)] * ND
    kact = [SB(f"f_ka{d}", [128, 512], BF16) for d in range(ND)]
    res = [SB(f"f_res{d}", [128, 512], F32) for d in range(ND)]
    pcol = [SB(f"f_pc{d}", [128, 1], F32) for d in range(ND)]
    lsc = ln_scratch(c, "f_ln", t1=diff)

    for q in range(QT):
        t0 = q * 512
        for d in range(ND):
            w_tt(P, "v", ALU.subtract, diff[:, 1:512], c.xt[d][:, t0 + 1:t0 + 512],
                 c.xt[d][:, t0:t0 + 511], dkey=key(diff),
                 rkeys=[c.xt[d][:, t0:t0 + 512]])
            w_stt(P, "v", xr[d][:, 1:512], diff[:, 1:512], mur[:, d:d + 1],
                  c.xt[d][:, t0:t0 + 511], ALU.mult, ALU.add,
                  dkey=key(xr[d]), rkeys=[key(diff), key(mur), c.xt[d][:, t0:t0 + 512]])
            w_stt(P, "v", xk[d][:, 1:512], diff[:, 1:512], muk[:, d:d + 1],
                  c.xt[d][:, t0:t0 + 511], ALU.mult, ALU.add,
                  dkey=key(xk[d]), rkeys=[key(diff), key(muk), c.xt[d][:, t0:t0 + 512]])
            if q == 0:
                w_ts(P, "v", xr[d][:, 0:1], c.xt[d][:, 0:1], mur[:, d:d + 1], None,
                     ALU.mult, dkey=key(xr[d]), rkeys=[c.xt[d][:, 0:1], key(mur)])
                w_ts(P, "v", xk[d][:, 0:1], c.xt[d][:, 0:1], muk[:, d:d + 1], None,
                     ALU.mult, dkey=key(xk[d]), rkeys=[c.xt[d][:, 0:1], key(muk)])
            else:
                w_tt(P, "v", ALU.subtract, diff[:, 0:1], c.xt[d][:, t0:t0 + 1],
                     pcol[d][:], dkey=key(diff), rkeys=[c.xt[d][:, t0:t0 + 1], key(pcol[d])])
                w_stt(P, "v", xr[d][:, 0:1], diff[:, 0:1], mur[:, d:d + 1],
                      pcol[d][:], ALU.mult, ALU.add,
                      dkey=key(xr[d]), rkeys=[key(diff), key(mur), key(pcol[d])])
                w_stt(P, "v", xk[d][:, 0:1], diff[:, 0:1], muk[:, d:d + 1],
                      pcol[d][:], ALU.mult, ALU.add,
                      dkey=key(xk[d]), rkeys=[key(diff), key(muk), key(pcol[d])])
            w_copy(P, "v", pcol[d][:], c.xt[d][:, t0 + 511:t0 + 512],
                   srckey=c.xt[d][:, t0 + 511:t0 + 512])
        for n in range(ND):
            ps = c.ps[n % 2]
            pk = f"psm{n % 2}"
            for d in range(ND):
                w_matmul(P, ps[:, 0:512], wr[:, d * D + n * 128:d * D + n * 128 + 128],
                         xr[d][:], d == 0, d == ND - 1, pk)
            w_act(P, rbuf[n][:], ps[:, 0:512], AF.Sigmoid, rkeys=[pk])
            ps2 = c.ps[2 + n % 2]
            pk2 = f"psm{2 + n % 2}"
            for d in range(ND):
                w_matmul(P, ps2[:, 0:512], wk[:, d * D + n * 128:d * D + n * 128 + 128],
                         xk[d][:], d == 0, d == ND - 1, pk2)
            P.op("v", lambda v, n=n, ps2=ps2: v.tensor_scalar_max(krel[n][:], ps2[:, 0:512], 0.0),
                 reads=[pk2], writes=[key(krel[n])])
            P.op("s", lambda s, n=n: s.activation(kact[n][:], krel[n][:], AF.Square),
                 reads=[key(krel[n])], writes=[key(kact[n])])
        for n in range(ND):
            ps = c.ps[n % 2]
            pk = f"psm{n % 2}"
            for d in range(ND):
                w_matmul(P, ps[:, 0:512], wv[:, d * D + n * 128:d * D + n * 128 + 128],
                         kact[d][:], d == 0, d == ND - 1, pk)
            w_stt(P, "v", res[n][:], ps[:, 0:512], 1.0, rbuf[n][:], ALU.mult, ALU.mult,
                  dkey=key(res[n]), rkeys=[pk, key(rbuf[n])])
            w_tt(P, "v", ALU.add, res[n][:], res[n][:], c.xt[n][:, t0:t0 + 512],
                 rkeys=[key(res[n]), c.xt[n][:, t0:t0 + 512]])
        ln_update(c, res, q, lsc)


def emit_l1a(c):
    P = c.P
    SB = c.SB
    NH, DH = 16, 64
    wslot = SB("a_wslot", [128, 8 * D], BF16)
    V65 = [SB(f"a_v{t}", [128, 16 * 65], BF16) for t in range(NT)]
    KT = [SB(f"a_kt{j}", [128, NTOK], BF16) for j in range(8)]
    QTc = [SB(f"a_qt{j}", [128, 512], BF16) for j in range(8)]
    OTc = [SB(f"a_ot{j}", [128, 512], BF16) for j in range(8)]
    PP = [SB(f"a_p{i}", [128, 512], BF16) for i in range(8)]
    sq = PP[7]
    onec_bf = SB("a_onec_bf", [128, 1], BF16)
    negb = SB("a_negb", [128, 8], F32)
    qm = SB("a_qm", [1, 8], F32)
    km = SB("a_km", [1, 8], F32)
    s11 = SB("a_s11", [1, 1], F32)
    denb = SB("a_denb", [64, 512], BF16)
    denr_bf = SB("a_denr", [1, 512], BF16)
    res = [PP[d][:].bitcast(F32)[:, 0:128] for d in range(ND)]
    res_keys = [key(PP[d]) for d in range(ND)]
    lsc = ln_scratch(c, "a_ln", W=128)
    cm01 = load_const(c, "s_cm01", "c_cm01")
    P.op("v", lambda v: v.memset(onec_bf[:], 1.0), writes=[key(onec_bf)])

    load_weight_bf16(c, wslot, c.inp["avT"], D, D)
    for t in range(NT):
        P.op("v", lambda v, t=t: v.memset(
            V65[t][:].rearrange("p (h c) -> p h c", c=65)[:, :, 64:65], 1.0),
            writes=[key(V65[t])])
        for nch in range(2):
            ps = c.ps[nch % 2]
            pk = f"psm{nch % 2}"
            for d in range(ND):
                w_matmul(P, ps[:, 0:512], c.xb[d][:, t * 128:(t + 1) * 128],
                         wslot[:, d * D + nch * 512:d * D + nch * 512 + 512],
                         d == 0, d == ND - 1, pk)
            for hh in range(8):
                h = nch * 8 + hh
                w_copy(P, "v" if hh % 2 == 0 else "s",
                       V65[t][:, h * 65:h * 65 + 64], ps[:, hh * 64:hh * 64 + 64],
                       dstkey=key(V65[t]), srckey=pk)

    load_weight_bf16(c, wslot, c.inp["akT"], D, D)
    for j in range(8):
        for ch in range(4):
            ps = c.ps[ch % 2]
            pk = f"psm{ch % 2}"
            for d in range(ND):
                w_matmul(P, ps[:, 0:512],
                         wslot[:, d * D + j * 128:d * D + j * 128 + 128],
                         c.xb[d][:, ch * 512:ch * 512 + 512],
                         d == 0, d == ND - 1, pk)
            w_copy(P, "v" if ch % 2 == 0 else "s", KT[j][:, ch * 512:ch * 512 + 512],
                   ps[:, 0:512], dstkey=key(KT[j]), srckey=pk)
        for ch in range(4):
            w_act(P, sq[:], KT[j][:, ch * 512:ch * 512 + 512], AF.Square,
                  dkey=key(sq), rkeys=[key(KT[j])])
            w_matmul(P, c.psr[0:1, 0:512], onec_bf[:], sq[:], True, True, "psr")
            w_reduce(P, "v", s11[:], c.psr[0:1, 0:512], ALU.max, srckey="psr")
            if ch == 0:
                w_copy(P, "v", km[:, j:j + 1], s11[:], dstkey=key(km), srckey=key(s11))
            else:
                P.op("v", lambda v, j=j: v.tensor_tensor(
                    km[:, j:j + 1], km[:, j:j + 1], s11[:], ALU.max),
                    reads=[key(km), key(s11)], writes=[key(km)])

    for qc in range(4):
        load_weight_bf16(c, wslot, c.inp["aqT"], D, D)
        for j in range(8):
            ps = c.ps[j % 2]
            pk = f"psm{j % 2}"
            for d in range(ND):
                w_matmul(P, ps[:, 0:512],
                         wslot[:, d * D + j * 128:d * D + j * 128 + 128],
                         c.xb[d][:, qc * 512:qc * 512 + 512],
                         d == 0, d == ND - 1, pk)
            w_copy(P, "v" if j % 2 == 0 else "s", QTc[j][:], ps[:, 0:512], srckey=pk)
            w_act(P, sq[:], QTc[j][:], AF.Square, dkey=key(sq), rkeys=[key(QTc[j])])
            w_matmul(P, c.psr[0:1, 0:512], onec_bf[:], sq[:], True, True, "psr")
            w_reduce(P, "v", s11[:], c.psr[0:1, 0:512], ALU.max, srckey="psr")
            w_tt(P, "v", ALU.add, s11[:], s11[:], km[:, j:j + 1],
                 dkey=key(s11), rkeys=[key(s11), key(km)])
            w_ts(P, "v", s11[:], s11[:], -1.0 / 16.0, None, ALU.mult)
            w_matmul(P, c.psx[0][:, 0:1], c.one1_f[:], s11[:], True, True, "psx0")
            w_copy(P, "v", negb[:, j:j + 1], c.psx[0][:, 0:1], dstkey=key(negb), srckey="psx0")

        load_weight_bf16(c, wslot, c.inp["aoT"], D, D)

        AVB = [(c.psx[0], "psx0"), (c.ps[0], "psm0"), (c.psx[1], "psx1"), (c.ps[1], "psm1")]

        def denscale(h):
            j = h // 2
            oh = (h % 2) * 64
            pav, pkav = AVB[h % 4]
            P.op("v", lambda v, pav=pav: v.reciprocal(c.psr[64:65, 0:512],
                 pav[64:65, 0:512]), reads=[pkav], writes=["psr"])
            P.op("v", lambda v: v.tensor_copy(denr_bf[:], c.psr[64:65, 0:512]),
                 reads=["psr"], writes=[key(denr_bf)])
            w_matmul(P, c.psr[0:64, 0:512], c.one1_bf[:, 0:64], denr_bf[:], True, True, "psr")
            w_copy(P, "s", denb[:], c.psr[0:64, 0:512], srckey="psr")
            w_stt(P, "v", OTc[j][oh:oh + 64, :],
                  pav[0:64, 0:512], 1.0, denb[:], ALU.mult, ALU.mult,
                  dkey=key(OTc[j]), rkeys=[pkav, key(denb)])

        kbmax = qc * 4 + 3
        for hp in range(8):
            h0, h1 = 2 * hp, 2 * hp + 1
            j = hp // 1 // 2 * 0 + hp // 2
            for kb in range(kbmax + 1):
                for hi, h in enumerate((h0, h1)):
                    jt = h // 2
                    oh = (h % 2) * 64
                    pst_b = c.ps[2 + hi]
                    pkst = f"psm{2 + hi}"
                    w_matmul(P, pst_b[:, 0:512],
                             KT[jt][oh:oh + 64, kb * 128:kb * 128 + 128],
                             QTc[jt][oh:oh + 64, :],
                             True, True, pkst)
                    pt = PP[4 * hi + kb % 4]
                    w_act(P, pt[:], pst_b[:, 0:512], AF.Exp,
                          bias=negb[:, jt:jt + 1], scale=0.125,
                          dkey=key(pt), rkeys=[pkst, key(negb)])
                    if kb >= qc * 4:
                        v_ = kb - qc * 4
                        w_tt(P, "v", ALU.mult, pt[:], pt[:],
                             cm01[:, 384 - 128 * v_:896 - 128 * v_],
                             rkeys=[key(pt), key(cm01)])
                    pav, pkav = AVB[h % 4]
                    w_matmul(P, pav[0:65, 0:512],
                             V65[kb][:, h * 65:h * 65 + 65], pt[:],
                             kb == 0, kb == kbmax, pkav)
            denscale(h0)
            denscale(h1)

        for sub in range(4):
            col0 = qc * 512 + sub * 128
            for d in range(ND):
                ps = c.ps[d % 2]
                pk = f"psm{d % 2}"
                for n in range(ND):
                    w_matmul(P, ps[:, 0:128],
                             wslot[:, n * D + d * 128:n * D + d * 128 + 128],
                             OTc[n][:, sub * 128:sub * 128 + 128],
                             n == 0, n == ND - 1, pk)
                P.op("v", lambda v, d=d, ps=ps, col0=col0: v.tensor_tensor(
                    res[d], ps[:, 0:128], c.xt[d][:, col0:col0 + 128], ALU.add),
                    reads=[pk, c.xt[d][:, col0:col0 + 128]], writes=[res_keys[d]])
            ln_update(c, res, None, lsc, W=128, col0=col0)


def emit_l0a(c):
    P = c.P
    SB = c.SB
    H, DK = 4, 256
    NC = 16
    id_bf = load_const(c, "s_id_bf", "c_id_bf")
    slneg = load_const(c, "s_slneg", "c_slneg")
    tu_f = load_const(c, "s_tu", "c_tu_f")
    gn_row = SB("d_gnrow", [1, DK], F32)
    w_dma(P, "y", None, gn_row[:], c.inp["dgn"])
    gnb = SB("d_gnb", [128, DK], F32)
    w_matmul(P, c.psx[0][:, 0:DK], c.one1_f[:], gn_row[:], True, True, "psx0")
    w_copy(P, "v", gnb[:], c.psx[0][:, 0:DK], srckey="psx0")
    bslot = SB("d_bslot", [128, 32], F32)
    w_dma(P, "y", None, bslot[:], c.inp["dwbeta"])
    beta = SB("d_beta", [128, 64], F32)
    for t in range(NT):
        for d in range(ND):
            w_matmul(P, c.psr[:, 0:4], c.xt[d][:, t * 128:(t + 1) * 128],
                     bslot[:, d * 4:(d + 1) * 4], d == 0, d == ND - 1, "psr")
        w_act(P, beta[:, t * 4:(t + 1) * 4], c.psr[:, 0:4], AF.Sigmoid,
              dkey=key(beta), rkeys=["psr"])

    OT = [SB(f"d_ot{j}", [128, NTOK], BF16) for j in range(8)]

    for pr in range(2):
        c.push_scope()
        QT = [[SB(f"d_qt{pr}_{hh}_{j}", [128, NTOK], BF16) for j in range(2)] for hh in range(2)]
        KT = [[SB(f"d_kt{pr}_{hh}_{j}", [128, NTOK], BF16) for j in range(2)] for hh in range(2)]
        ktok = [SB(f"d_ktok{pr}_{t}", [128, 2 * DK], BF16) for t in range(NT)]
        vtok = [SB(f"d_vtok{pr}_{t}", [128, 2 * DK], BF16) for t in range(NT)]
        nsq = SB(f"d_nsq{pr}", [128, 1], F32)

        bankP = [c.ps[1], c.ps[3]]
        bankX = [c.ps[0], c.ps[2]]
        kPa = ["pP0", "pP1"]
        kPb = ["pP0", "pP1"]
        kXa = ["pX0", "pX1"]
        kXb = ["pX0", "pX1"]

        c.push_scope()
        qslot = SB(f"d_qslot{pr}", [128, 8 * DK], BF16)
        kslot = SB(f"d_kslot{pr}", [128, 8 * DK], BF16)
        vslot = qslot
        qn = SB(f"d_qn{pr}", [128, DK], BF16)
        sqd_t = SB(f"d_sqd{pr}", [128, DK], F32)
        sqd = sqd_t[:]
        for hh in range(2):
            h = 2 * pr + hh
            for slot, wn in ((qslot, "dwq"), (kslot, "dwk")):
                w_dma(P, "g", None, slot[:], c.inp[wn][h * 128:(h + 1) * 128, :],
                      dstkey=key(slot), srckey=key(c.inp[wn]))
            for which, slot in (("q", qslot), ("k", kslot), ("v", vslot)):
                if which == "v":
                    w_dma(P, "g", None, vslot[:], c.inp["dwv"][h * 128:(h + 1) * 128, :],
                          dstkey=key(vslot), srckey=key(c.inp["dwv"]))
                for t in range(NT):
                    ps = c.ps[t % 4]
                    pk = f"psm{t % 4}"
                    for d in range(ND):
                        w_matmul(P, ps[:, 0:DK], c.xb[d][:, t * 128:(t + 1) * 128],
                                 slot[:, d * DK:(d + 1) * DK], d == 0, d == ND - 1, pk)
                    if which == "v":
                        w_copy(P, "v", vtok[t][:, hh * DK:(hh + 1) * DK], ps[:, 0:DK],
                               dstkey=key(vtok[t]), srckey=pk)
                        continue
                    w_act(P, sqd, ps[:, 0:DK], AF.Square, accum=nsq[:],
                          dkey=key(sqd_t), rkeys=[pk])
                    P.op("v", lambda v, nsq=nsq: v.reciprocal(nsq[:], nsq[:]),
                         reads=[key(nsq)], writes=[key(nsq)])
                    P.op("s", lambda s, nsq=nsq: s.activation(nsq[:], nsq[:], AF.Sqrt),
                         reads=[key(nsq)], writes=[key(nsq)])
                    dst_tok = qn if which == "q" else ktok[t]
                    dsl = dst_tok[:] if which == "q" else dst_tok[:, hh * DK:(hh + 1) * DK]
                    P.op("v", lambda v, dsl=dsl, ps=ps, nsq=nsq, pk=pk: v.tensor_single_scalar(
                        dsl, ps[:, 0:DK], nsq[:], ALU.mult),
                        reads=[pk, key(nsq)], writes=[key(dst_tok)])
                    dstT = QT[hh] if which == "q" else KT[hh]
                    for j in range(2):
                        tb = (2 * t + j) % 3
                        if tb == 0:
                            tps, tkey = c.pst[:, j * 128:j * 128 + 128], "pst"
                        else:
                            tps = c.psx[tb - 1][:].bitcast(BF16)[:, 0:128]
                            tkey = f"psx{tb - 1}"
                        src_ap = (qn[:, j * 128:(j + 1) * 128] if which == "q"
                                  else ktok[t][:, hh * DK + j * 128:hh * DK + (j + 1) * 128])
                        w_transpose(P, tps, src_ap, id_bf[:], tkey)
                        w_copy(P, "s", dstT[j][:, t * 128:(t + 1) * 128], tps,
                               dstkey=key(dstT[j]), srckey=tkey)
        c.pop_scope()
        S32 = [[SB(f"d_s32_{pr}_{hh}_{j}", [128, DK], F32) for j in range(2)] for hh in range(2)]
        Sbf = [[SB(f"d_sbf_{pr}_{hh}_{j}", [128, DK], BF16) for j in range(2)] for hh in range(2)]
        tmpf0 = SB(f"d_tmpf{pr}", [128, 256], F32)
        tmpf = [tmpf0, tmpf0]
        Pbf = [SB(f"d_pbf{pr}_{hh}", [128, 128], BF16) for hh in range(2)]
        PTbf = [SB(f"d_ptbf{pr}_{hh}", [128, 128], BF16) for hh in range(2)]
        TTbf = [SB(f"d_ttbf{pr}_{hh}", [128, 128], BF16) for hh in range(2)]
        Xbf = [SB(f"d_xbf{pr}_{hh}", [128, DK], BF16) for hh in range(2)]
        oto = [SB(f"d_oto{pr}_{hh}", [128, DK], BF16) for hh in range(2)]
        sqd = tmpf0[:, 0:256]


        def prefetch_MK(ci2):
            bank = c.psx[ci2 % 2]
            bkey = f"psx{ci2 % 2}"
            cs2 = slice(ci2 * 128, ci2 * 128 + 128)
            for hh in range(2):
                off = 256 * hh
                for j in range(2):
                    w_matmul(P, bank[:, off:off + 128], KT[hh][j][:, cs2],
                             KT[hh][j][:, cs2], j == 0, j == 1, bkey)
                for j in range(2):
                    w_matmul(P, bank[:, off + 128:off + 256], KT[hh][j][:, cs2],
                             QT[hh][j][:, cs2], j == 0, j == 1, bkey)
        prefetch_MK(0)
        for ci in range(NC):
            cs = slice(ci * 128, ci * 128 + 128)
            first = ci == 0
            mkbank = c.psx[ci % 2]
            mkkey = f"psx{ci % 2}"
            for hh in range(2):
                h = 2 * pr + hh
                bcol = beta[:, ci * 4 + h:ci * 4 + h + 1]
                if not first:
                    for j in range(2):
                        w_copy(P, "v", Sbf[hh][j][:], S32[hh][j][:])
                off = 256 * hh
                w_tt(P, "v", ALU.mult, TTbf[hh][:], mkbank[:, off + 128:off + 256], tu_f[:],
                     dkey=key(TTbf[hh]), rkeys=[mkkey, key(tu_f)])
                w_stt(P, "v", Pbf[hh][:], mkbank[:, off:off + 128], bcol, slneg[:],
                      ALU.mult, ALU.mult,
                      dkey=key(Pbf[hh]), rkeys=[mkkey, key(beta), key(slneg)])
                w_transpose(P, c.pst[:, (4 + hh) * 128:(5 + hh) * 128], Pbf[hh][:],
                            id_bf[:], "pst")
                w_copy(P, "v", PTbf[hh][:], c.pst[:, (4 + hh) * 128:(5 + hh) * 128],
                       dstkey=key(PTbf[hh]), srckey="pst")
                if first:
                    P.op("v", lambda v, hh=hh, ci=ci, bcol=bcol, Xbf=Xbf, vtok=vtok:
                        v.tensor_single_scalar(
                        Xbf[hh][:], vtok[ci][:, hh * DK:(hh + 1) * DK], bcol, ALU.mult),
                        reads=[key(vtok[ci]), key(beta)], writes=[key(Xbf[hh])])
                else:
                    for j in range(2):
                        w_matmul(P, bankX[hh][:, 0:DK], KT[hh][j][:, cs], Sbf[hh][j][:],
                                 j == 0, j == 1, kXa[hh])
                    P.op("v", lambda v, hh=hh, ci=ci, tmpf=tmpf, vtok=vtok, bankX=bankX:
                        v.tensor_tensor(
                        tmpf[hh][:, 0:256], vtok[ci][:, hh * DK:(hh + 1) * DK],
                        bankX[hh][:, 0:DK], ALU.subtract),
                        reads=[key(vtok[ci]), kXa[hh]], writes=[key(tmpf[hh])])
                    P.op("v", lambda v, hh=hh, bcol=bcol, Xbf=Xbf, tmpf=tmpf:
                        v.tensor_single_scalar(
                        Xbf[hh][:], tmpf[hh][:, 0:256], bcol, ALU.mult),
                        reads=[key(tmpf[hh]), key(beta)], writes=[key(Xbf[hh])])
            for lev in range(7):
                for hh in range(2):
                    w_matmul(P, bankX[hh][:, 0:DK], id_bf[:], Xbf[hh][:], True, False, kXa[hh])
                    w_matmul(P, bankX[hh][:, 0:DK], PTbf[hh][:], Xbf[hh][:], False, True, kXa[hh])
                for hh in range(2):
                    w_copy(P, "v", Xbf[hh][:], bankX[hh][:, 0:DK], srckey=kXa[hh])
                    if lev < 6:
                        w_matmul(P, bankP[hh][:, 0:128], PTbf[hh][:], Pbf[hh][:],
                                 True, True, kPa[hh])
                        w_matmul(P, bankP[hh][:, 256:384], Pbf[hh][:], PTbf[hh][:],
                                 True, True, kPb[hh])
                for hh in range(2):
                    if lev < 6:
                        w_copy(P, "s", Pbf[hh][:], bankP[hh][:, 0:128], srckey=kPa[hh])
                        w_copy(P, "s", PTbf[hh][:], bankP[hh][:, 256:384], srckey=kPb[hh])
                if lev == 2 and ci + 1 < NC:
                    prefetch_MK(ci + 1)
            for hh in range(2):
                for j in range(2):
                    kk = kPa[hh] if j == 0 else kPb[hh]
                    off = 0 if j == 0 else 256
                    w_matmul(P, bankP[hh][:, off:off + DK],
                             ktok[ci][:, hh * DK + j * 128:hh * DK + (j + 1) * 128],
                             Xbf[hh][:], True, True, kk)
                    if first:
                        w_copy(P, "v", S32[hh][j][:], bankP[hh][:, off:off + DK], srckey=kk)
                    else:
                        P.op("v", lambda v, hh=hh, j=j, off=off, S32=S32, bankP=bankP:
                            v.tensor_tensor(
                            S32[hh][j][:], S32[hh][j][:], bankP[hh][:, off:off + DK],
                            ALU.add), reads=[key(S32[hh][j]), kk], writes=[key(S32[hh][j])])
            for hh in range(2):
                h = 2 * pr + hh
                obank = c.psr if hh == 0 else bankP[1]
                obkey = "psr" if hh == 0 else kPa[1]
                osl = obank[:, 0:DK] if hh == 0 else obank[:, 256:512]
                if first:
                    w_matmul(P, osl, TTbf[hh][:], Xbf[hh][:], True, True, obkey)
                else:
                    for j in range(2):
                        w_matmul(P, osl, QT[hh][j][:, cs], Sbf[hh][j][:],
                                 j == 0, False, obkey)
                    w_matmul(P, osl, TTbf[hh][:], Xbf[hh][:], False, True, obkey)
                w_act(P, sqd, osl, AF.Square, accum=nsq[:],
                      dkey=key(tmpf[0]), rkeys=[obkey])
                w_ts(P, "v", nsq[:], nsq[:], 1.0 / DK, 1e-5, ALU.mult, ALU.add,
                     dkey=key(nsq), rkeys=[key(nsq)])
                P.op("v", lambda v, nsq=nsq: v.reciprocal(nsq[:], nsq[:]),
                     reads=[key(nsq)], writes=[key(nsq)])
                P.op("s", lambda s, nsq=nsq: s.activation(nsq[:], nsq[:], AF.Sqrt),
                     reads=[key(nsq)], writes=[key(nsq)])
                w_stt(P, "v", oto[hh][:], osl, nsq[:], gnb[:],
                      ALU.mult, ALU.mult, dkey=key(oto[hh]),
                      rkeys=[obkey, key(nsq), key(gnb)])
                for j in range(2):
                    if j == 0:
                        tps = c.pst[:, (2 * hh) * 128:(2 * hh) * 128 + 128]
                        tkey = "pst"
                    else:
                        tps = c.psx[ci % 2][:].bitcast(BF16)[:, hh * 128:hh * 128 + 128]
                        tkey = f"psx{ci % 2}"
                    w_transpose(P, tps, oto[hh][:, j * 128:(j + 1) * 128], id_bf[:], tkey)
                    w_copy(P, "s", OT[2 * h + j][:, cs], tps,
                           dstkey=key(OT[2 * h + j]), srckey=tkey)
        c.pop_scope()

    wo = SB("d_wo", [128, 8 * D], BF16)
    load_weight_bf16(c, wo, c.inp["dwo"], D, D)
    res = [SB(f"d_res{d}", [128, 512], F32) for d in range(ND)]
    lsc = ln_scratch(c, "d_ln", W=512)
    for ch in range(4):
        for d in range(ND):
            ps = c.ps[d % 2]
            pk = f"psm{d % 2}"
            for n in range(ND):
                w_matmul(P, ps[:, 0:512],
                         wo[:, n * D + d * 128:n * D + d * 128 + 128],
                         OT[n][:, ch * 512:ch * 512 + 512],
                         n == 0, n == ND - 1, pk)
            P.op("v", lambda v, d=d, ps=ps, ch=ch: v.tensor_tensor(
                res[d][:], ps[:, 0:512], c.xt[d][:, ch * 512:ch * 512 + 512], ALU.add),
                reads=[pk, c.xt[d][:, ch * 512:ch * 512 + 512]], writes=[key(res[d])])
        ln_update(c, res, None, lsc, W=512, col0=ch * 512)


L0F_NQ = 4


def emit_l0f(c):
    P = c.P
    SB = c.SB
    E = 8
    id_f = load_const(c, "s_id_f", "c_id_f")
    oner_f = load_const(c, "s_oner_f", "c_oner_f")
    slotA = SB("m_slotA", [128, 8 * D], BF16)
    slotB = SB("m_slotB", [128, 8 * D], BF16)
    zh = [SB(f"m_zh{i}", [128, 512], BF16) for i in range(16)]
    g = [SB(f"m_g{i}", [128, 512], BF16) for i in range(8)]
    xw = [SB(f"m_xw{i}", [128, 512], BF16) for i in range(8)]
    yq = [SB(f"m_y{i}", [128, 512], F32) for i in range(8)]
    gws = SB("m_gws", [128, 64], F32)
    b1c = SB("m_b1c", [128, 64], F32)
    wselq = SB("m_wsel", [128, 32], F32)
    keepq = SB("m_keep", [128, 32], F32)
    pcol = SB("m_pcol", [128, 8], F32)
    m1c = SB("m_m1c", [128, 1], F32)
    rsc = SB("m_rsc", [128, 1], F32)
    wrow = SB("m_wrow", [1, 512], F32)
    b2c = SB("m_b2c", [128, 64], F32)
    b3s = SB("m_b3s", [1, D], BF16)
    mrow_bf = SB("m_mrowbf", [1, 512], BF16)
    x3m = [SB(f"m_x3m{d}", [128, 512], BF16) for d in range(ND)]
    prod = SB("m_prod", [128, 512], F32)
    lsc = ln_scratch(c, "m_ln", W=512)
    w_dma(P, "y", None, gws[:], c.inp["gwT"])
    w_dma(P, "y", None, b1c[:], c.inp["b1c"])
    w_dma(P, "y", None, b2c[:], c.inp["b2c"])

    w1T, w2T, w3T = c.inp["w1T"], c.inp["w2T"], c.inp["w3T"]
    sw1T, sw2T, sw3T = c.inp["sw1T"], c.inp["sw2T"], c.inp["sw3T"]

    def load_slot(slot, dram, blk):
        w_dma(P, "g", None, slot[:], dram[blk * 128:(blk + 1) * 128, :],
              dstkey=key(slot), srckey=key(dram))

    for tq in range(L0F_NQ):
        t0 = tq * 512
        for tt in range(4):
            ti = tq * 4 + tt
            pr = c.psr
            for d in range(ND):
                w_matmul(P, pr[:, 0:8], c.xt[d][:, ti * 128:(ti + 1) * 128],
                         gws[:, d * 8:(d + 1) * 8], d == 0, d == ND - 1, "psr")
            w_reduce(P, "v", m1c[:], pr[:, 0:8], ALU.max, srckey="psr", negate=True)
            w_act(P, pcol[:], pr[:, 0:8], AF.Exp, bias=m1c[:], accum=rsc[:],
                  dkey=key(pcol), rkeys=["psr", key(m1c)])
            P.op("v", lambda v: v.reciprocal(rsc[:], rsc[:]),
                 reads=[key(rsc)], writes=[key(rsc)])
            w_ts(P, "v", pcol[:], pcol[:], rsc[:], None, ALU.mult,
                 dkey=key(pcol), rkeys=[key(pcol), key(rsc)])
            w_reduce(P, "v", m1c[:], pcol[:], ALU.max, srckey=key(pcol), dkey=key(m1c))
            w_ts(P, "v", keepq[:, tt * 8:(tt + 1) * 8], pcol[:], m1c[:], None,
                 ALU.is_ge, dkey=key(keepq), rkeys=[key(pcol), key(m1c)])
            w_stt(P, "v", wselq[:, tt * 8:(tt + 1) * 8], keepq[:, tt * 8:(tt + 1) * 8],
                  -1e30, pcol[:], ALU.mult, ALU.add,
                  dkey=key(wselq), rkeys=[key(keepq), key(pcol)])
            w_reduce(P, "v", m1c[:], wselq[:, tt * 8:(tt + 1) * 8], ALU.max,
                     srckey=key(wselq), dkey=key(m1c))
            w_ts(P, "v", keepq[:, tt * 8:(tt + 1) * 8], pcol[:], m1c[:], None,
                 ALU.is_ge, dkey=key(keepq), rkeys=[key(pcol), key(m1c)])
            w_tt(P, "v", ALU.mult, wselq[:, tt * 8:(tt + 1) * 8], pcol[:],
                 keepq[:, tt * 8:(tt + 1) * 8],
                 dkey=key(wselq), rkeys=[key(pcol), key(keepq)])

        for ih in range(2):
            load_slot(slotA, sw1T, ih)
            load_slot(slotB, sw2T, ih)
            for i8 in range(8):
                i = ih * 8 + i8
                ps = c.ps[i8 % 2]
                pk = f"psm{i8 % 2}"
                for d in range(ND):
                    w_matmul(P, ps[:, 0:512],
                             slotA[:, d * 1024 + i8 * 128:d * 1024 + i8 * 128 + 128],
                             c.xb[d][:, t0:t0 + 512], d == 0, d == ND - 1, pk)
                w_act(P, g[i8][:], ps[:, 0:512], AF.Silu, dkey=key(g[i8]), rkeys=[pk])
            for i8 in range(8):
                i = ih * 8 + i8
                ps = c.ps[i8 % 2]
                pk = f"psm{i8 % 2}"
                for d in range(ND):
                    w_matmul(P, ps[:, 0:512],
                             slotB[:, d * 1024 + i8 * 128:d * 1024 + i8 * 128 + 128],
                             c.xb[d][:, t0:t0 + 512], d == 0, d == ND - 1, pk)
                w_stt(P, "v", zh[i][:], ps[:, 0:512], 1.0, g[i8][:], ALU.mult, ALU.mult,
                      dkey=key(zh[i]), rkeys=[pk, key(g[i8])])
            if ih == 0:
                load_slot(slotA, sw1T, 1)
        load_slot(slotA, sw3T, 0)
        load_slot(slotB, sw3T, 1)
        for d in range(ND):
            ps = c.ps[d % 2]
            pk = f"psm{d % 2}"
            for i in range(8):
                w_matmul(P, ps[:, 0:512],
                         slotA[:, i * 1024 + d * 128:i * 1024 + d * 128 + 128],
                         zh[i][:], i == 0, False, pk)
            for i in range(8):
                w_matmul(P, ps[:, 0:512],
                         slotB[:, i * 1024 + d * 128:i * 1024 + d * 128 + 128],
                         zh[8 + i][:], False, i == 7, pk)
            w_copy(P, "v" if d % 2 == 0 else "s", yq[d][:], ps[:, 0:512], srckey=pk)

        load_slot(slotA, w1T, 0)
        load_slot(slotB, w3T, 0)
        for e in range(E):
            sl1, sl3 = slotA, slotB
            for tt in range(4):
                w_matmul(P, c.psr[0:1, 0:128], wselq[:, tt * 8 + e:tt * 8 + e + 1],
                         id_f[:], True, True, "psr")
                w_copy(P, "v", wrow[0:1, tt * 128:(tt + 1) * 128], c.psr[0:1, 0:128],
                       dstkey=key(wrow), srckey="psr")
                w_matmul(P, c.psr[0:1, 0:128], keepq[:, tt * 8 + e:tt * 8 + e + 1],
                         id_f[:], True, True, "psr")
                w_copy(P, "v", mrow_bf[0:1, tt * 128:(tt + 1) * 128], c.psr[0:1, 0:128],
                       dstkey=key(mrow_bf), srckey="psr")
            w_dma(P, "g", None, b3s[:], c.inp["b3r"][e:e + 1, :])
            w_matmul(P, c.psx[0][:, 0:512], c.one1_f[:], wrow[0:1, 0:512], True, True, "psx0")
            for d in range(ND):
                w_tt(P, "v", ALU.mult, xw[d][:], c.xt[d][:, t0:t0 + 512],
                     c.psx[0][:, 0:512], dkey=key(xw[d]),
                     rkeys=[c.xt[d][:, t0:t0 + 512], "psx0"])
            for d in range(ND):
                ps3 = c.ps[2 + d % 2]
                pk3 = f"psm{2 + d % 2}"
                for dd in range(ND):
                    w_matmul(P, ps3[:, 0:512],
                             sl3[:, dd * 1024 + d * 128:dd * 1024 + d * 128 + 128],
                             xw[dd][:], dd == 0, False, pk3)
                w_matmul(P, ps3[:, 0:512], b3s[0:1, d * 128:(d + 1) * 128],
                         mrow_bf[0:1, 0:512], False, True, pk3)
                w_copy(P, "s", x3m[d][:], ps3[:, 0:512], srckey=pk3)
            load_slot(sl3, w2T, e)
            for h in range(ND):
                ps = c.ps[h % 2]
                pk = f"psm{h % 2}"
                for d in range(ND):
                    w_matmul(P, ps[:, 0:512],
                             sl1[:, d * 1024 + h * 128:d * 1024 + h * 128 + 128],
                             xw[d][:], d == 0, d == ND - 1, pk)
                w_act(P, g[h][:], ps[:, 0:512], AF.Silu, bias=b1c[:, e * 8 + h:e * 8 + h + 1],
                      dkey=key(g[h]), rkeys=[pk, key(b1c)])
            if e + 1 < E:
                load_slot(sl1, w1T, e + 1)
            for d in range(ND):
                ps2 = c.ps[d % 2]
                pk2 = f"psm{d % 2}"
                for h in range(ND):
                    w_matmul(P, ps2[:, 0:512],
                             sl3[:, h * 1024 + d * 128:h * 1024 + d * 128 + 128],
                             g[h][:], h == 0, h == ND - 1, pk2)
                w_stt(P, "v", prod[:], ps2[:, 0:512], b2c[:, e * 8 + d:e * 8 + d + 1],
                      x3m[d][:], ALU.add, ALU.mult,
                      dkey=key(prod), rkeys=[pk2, key(b2c), key(x3m[d])])
                P.op("v", lambda v, d=d: v.tensor_tensor(yq[d][:], yq[d][:], prod[:],
                     ALU.add), reads=[key(yq[d]), key(prod)], writes=[key(yq[d])])
            if e + 1 < E:
                load_slot(slotB, w3T, e + 1)

        for d in range(ND):
            P.op("g", lambda gp, d=d, t0=t0: gp.tensor_tensor(yq[d][:], yq[d][:],
                 c.xt[d][:, t0:t0 + 512], ALU.add),
                 reads=[key(yq[d]), c.xt[d][:, t0:t0 + 512]], writes=[key(yq[d])])
        ln_update(c, [y[:] for y in yq], None, lsc, W=512, col0=t0)


STAGE_INPUTS = {
    "l1f": [("wrT", [128, 8 * D], F32), ("wkT", [128, 8 * D], F32), ("wvT", [128, 8 * D], F32),
            ("mur", [128, 8], F32), ("muk", [128, 8], F32)],
    "l1a": [("aqT", [128, 8 * D], F32), ("akT", [128, 8 * D], F32), ("avT", [128, 8 * D], F32),
            ("aoT", [128, 8 * D], F32)],
    "l0a": [("dwq", [512, 8 * 256], F32), ("dwk", [512, 8 * 256], F32), ("dwv", [512, 8 * 256], F32),
            ("dwbeta", [128, 32], F32), ("dwo", [128, 8 * D], F32), ("dgn", [1, 256], F32)],
    "l0f": [("gwT", [128, 64], F32), ("b1c", [128, 64], F32), ("b2c", [128, 64], F32),
            ("b3r", [8, D], F32),
            ("w1T", [8 * 128, 8 * D], F32), ("w2T", [8 * 128, 8 * D], F32),
            ("w3T", [8 * 128, 8 * D], F32),
            ("sw1T", [2 * 128, 8 * D], F32), ("sw2T", [2 * 128, 8 * D], F32),
            ("sw3T", [2 * 128, 8 * D], F32)],
}
STAGE_EMIT = {"l1f": emit_l1f, "l1a": emit_l1a, "l0f": emit_l0f, "l0a": emit_l0a}


def build_program(stages):
    nc = bass.Bass("TRN2", target_bir_lowering=False)
    es = ExitStack()
    need = [("xin", [D, NTOK], F32)] + const_inputs_spec()
    for st in stages:
        need += STAGE_INPUTS[st]
    with es:
        P = Prog(nc, es)
        c = make_ctx(nc, es, P, need)
        blk = es.enter_context(nc.Block())
        emit_load_stream(c)
        for st in stages:
            c.push_scope()
            STAGE_EMIT[st](c)
            c.pop_scope()
        emit_store_stream(c)
        P.finalize(blk)
    return nc


def tile128(a):
    a = np.ascontiguousarray(a)
    R, C = a.shape
    return np.ascontiguousarray(a.reshape(R // 128, 128, C).transpose(1, 0, 2).reshape(128, -1))


def host_inputs_for(stages, inputs):
    x = np.ascontiguousarray(inputs["L0_attn_x"])
    consts = const_inputs_host()
    maps = []
    stage_w = {}
    if "l0a" in stages:
        stage_w.update({
            "dwq": np.concatenate([tile128(np.asarray(inputs["L0_attn_Wq"])[:, h * 256:(h + 1) * 256]) for h in range(4)], axis=0).reshape(512, 8 * 256),
            "dwk": np.concatenate([tile128(np.asarray(inputs["L0_attn_Wk"])[:, h * 256:(h + 1) * 256]) for h in range(4)], axis=0).reshape(512, 8 * 256),
            "dwv": np.concatenate([tile128(np.asarray(inputs["L0_attn_Wv"])[:, h * 256:(h + 1) * 256]) for h in range(4)], axis=0).reshape(512, 8 * 256),
            "dwbeta": np.ascontiguousarray(np.asarray(inputs["L0_attn_Wbeta"]).reshape(8, 128, 4).transpose(1, 0, 2).reshape(128, 32)),
            "dwo": tile128(np.ascontiguousarray(inputs["L0_attn_Wo"])),
            "dgn": np.ascontiguousarray(np.asarray(inputs["L0_attn_g_norm"]).reshape(1, 256)),
        })
    if "l0f" in stages:
        stage_w.update({
            "gwT": np.ascontiguousarray(np.asarray(inputs["L0_ffn_gate_w"]).T.reshape(8, 128, 8).transpose(1, 0, 2).reshape(128, 64)),
            "b1c": np.concatenate([np.ascontiguousarray(
                np.asarray(inputs["L0_ffn_b1"][e]).reshape(8, 128).T) for e in range(8)], axis=1),
            "b2c": np.concatenate([np.ascontiguousarray(
                np.asarray(inputs["L0_ffn_b2"][e]).reshape(8, 128).T) for e in range(8)], axis=1),
            "b3r": np.ascontiguousarray(inputs["L0_ffn_b3"]),
            "w1T": np.concatenate([tile128(np.asarray(inputs["L0_ffn_w1"][e]).T)
                                   for e in range(8)], axis=0),
            "w2T": np.concatenate([tile128(np.asarray(inputs["L0_ffn_w2"][e]).T)
                                   for e in range(8)], axis=0),
            "w3T": np.concatenate([tile128(np.asarray(inputs["L0_ffn_w3"][e]).T)
                                   for e in range(8)], axis=0),
            "sw1T": np.concatenate([tile128(np.asarray(inputs["L0_ffn_sw1"]).T[:, i * 1024:(i + 1) * 1024])
                                    for i in range(2)], axis=0).reshape(256, 8 * D),
            "sw2T": np.concatenate([tile128(np.asarray(inputs["L0_ffn_sw2"]).T[:, i * 1024:(i + 1) * 1024])
                                    for i in range(2)], axis=0).reshape(256, 8 * D),
            "sw3T": np.concatenate([tile128(np.asarray(inputs["L0_ffn_sw3"]).T[i * 1024:(i + 1) * 1024, :])
                                    for i in range(2)], axis=0).reshape(256, 8 * D),
        })
    if "l1a" in stages:
        stage_w.update({
            "aqT": tile128(np.asarray(inputs["L1_attn_Wq"]).reshape(1024, 1024)),
            "akT": tile128(np.asarray(inputs["L1_attn_Wk"]).reshape(1024, 1024)),
            "avT": tile128(np.asarray(inputs["L1_attn_Wv"]).reshape(1024, 1024)),
            "aoT": tile128(np.asarray(inputs["L1_attn_Wout"]).reshape(1024, 1024)),
        })
    if "l1f" in stages:
        stage_w.update({
            "wrT": tile128(np.asarray(inputs["L1_ffn_W_r"]).T),
            "wkT": tile128(np.asarray(inputs["L1_ffn_W_k"]).T),
            "wvT": tile128(np.asarray(inputs["L1_ffn_W_v"]).T),
            "mur": np.ascontiguousarray(inputs["L1_ffn_mu_r"].reshape(8, 128).T),
            "muk": np.ascontiguousarray(inputs["L1_ffn_mu_k"].reshape(8, 128).T),
        })
    for b in range(N_CORES):
        m = {"xin": np.ascontiguousarray(x[b].T)}
        m.update(consts)
        m.update(stage_w)
        maps.append(m)
    return maps


def kernel(**inputs):
    stages = ["l0a", "l0f", "l1a", "l1f"]
    nc = build_program(stages)
    maps = host_inputs_for(stages, inputs)
    res = run_bass_kernel_spmd(nc, maps, core_ids=list(range(N_CORES)))
    out = np.stack([np.ascontiguousarray(r["out"].T) for r in res.results], axis=0)
    return out.astype(np.float32)


# version 26
# speedup vs baseline: 1.0084x; 1.0050x over previous
import numpy as np
from contextlib import ExitStack

import concourse.bass as bass
import concourse.mybir as mybir
from concourse.bass_utils import run_bass_kernel_spmd

F32 = mybir.dt.float32
BF16 = mybir.dt.bfloat16
AF = mybir.ActivationFunctionType
ALU = mybir.AluOpType
AX = mybir.AxisListType

N_CORES = 8
NTOK = 2048
D = 1024
NT = NTOK // 128
ND = D // 128


class Prog:
    CHAINS = ("t", "v", "s", "g", "y")

    def __init__(self, nc, es):
        self.nc = nc
        self.es = es
        self.ops = {e: [] for e in self.CHAINS}
        self.cnt = {e: 0 for e in self.CHAINS}
        self.sems = {}
        for e in self.CHAINS:
            self.sems["ch_" + e] = es.enter_context(nc.semaphore("ch_" + e))
        self.dma_issued = {}
        self.last_write = {}
        self.readers = {}
        self.clock = {e: {} for e in self.CHAINS}
        self.snap = {}

    def dma_sem(self, name):
        if name not in self.dma_issued:
            self.sems[name] = self.es.enter_context(self.nc.semaphore(name))
            self.dma_issued[name] = 0
        return name

    def _waits_for(self, e, reads, writes):
        toks = []
        for k in reads:
            t = self.last_write.get(k)
            if t:
                toks.append(t)
        for k in writes:
            t = self.last_write.get(k)
            if t:
                toks.append(t)
            toks.extend(self.readers.get(k, ()))
        need = {}
        clk = self.clock[e]
        for (sn, v) in toks:
            if sn in self.dma_issued:
                v = 16 * self.dma_issued[sn]
            if e == "t" and sn == "ch_t":
                continue
            if clk.get(sn, -1) < v:
                need[sn] = max(need.get(sn, 0), v)
                s2 = self.snap.get((sn, v))
                if s2:
                    for sn2, v2 in s2.items():
                        if clk.get(sn2, -1) < v2:
                            clk[sn2] = v2
        return need

    def _record(self, e, reads, writes, token):
        for k in writes:
            self.last_write[k] = token
            self.readers[k] = []
        for k in reads:
            self.readers.setdefault(k, []).append(token)

    def op(self, e, fn, reads=(), writes=(), skip_dep_check=False, inc=1):
        reads = norm_keys(reads)
        writes = norm_keys(writes)
        need = {} if skip_dep_check else self._waits_for(e, reads, writes)
        self.cnt[e] += 1
        myval = self.cnt[e]
        mysem = "ch_" + e
        sems = self.sems

        def emit(eng, need=need, fn=fn, mysem=mysem):
            for sn in sorted(need):
                eng.wait_ge(sems[sn], need[sn])
            ins = fn(eng)
            ins.then_inc(sems[mysem], 1)
        self.ops[e].append(emit)
        for sn, v in need.items():
            self.clock[e][sn] = max(self.clock[e].get(sn, -1), v)
        token = (mysem, myval)
        snap = dict(self.clock[e])
        snap[mysem] = myval - 1
        self.snap[token] = snap
        self._record(e, reads, writes, token)
        return token

    def dma(self, e, semname, fn, reads=(), writes=()):
        self.dma_sem(semname)
        reads = norm_keys(reads)
        writes = norm_keys(writes)
        need = self._waits_for(e, reads, writes)
        self.dma_issued[semname] += 1
        myval = 16 * self.dma_issued[semname]
        sems = self.sems

        def emit(eng, need=need, fn=fn, semname=semname):
            for sn in sorted(need):
                eng.wait_ge(sems[sn], need[sn])
            fn(eng).then_inc(sems[semname], 16)
        self.ops[e].append(emit)
        for sn, v in need.items():
            self.clock[e][sn] = max(self.clock[e].get(sn, -1), v)
        token = (semname, myval)
        self.snap[token] = dict(self.clock[e])
        self._record(e, reads, writes, token)
        return token

    def dma_batch(self, e, semname, fns, reads=(), writes=()):
        self.dma_sem(semname)
        reads = norm_keys(reads)
        writes = norm_keys(writes)
        need = self._waits_for(e, reads, writes)
        sems = self.sems

        def emit(eng, need=need, fns=fns, semname=semname):
            for sn in sorted(need):
                eng.wait_ge(sems[sn], need[sn])
            for fn in fns:
                fn(eng).then_inc(sems[semname], 16)
        self.ops[e].append(emit)
        for sn, v in need.items():
            self.clock[e][sn] = max(self.clock[e].get(sn, -1), v)
        self.dma_issued[semname] += len(fns)
        token = (semname, 16 * self.dma_issued[semname])
        self.snap[token] = dict(self.clock[e])
        self._record(e, reads, writes, token)
        return token

    def full_barrier(self):
        snap = [("ch_" + e, self.cnt[e]) for e in self.CHAINS]
        snap += [(s, 16 * n) for s, n in self.dma_issued.items()]
        for e in self.CHAINS:
            for sn, v in snap:
                if sn == "ch_" + e or v <= 0:
                    continue
                if self.clock[e].get(sn, -1) < v:
                    self.barrier_wait(e, sn, v)
                    self.clock[e][sn] = v

    def barrier_wait(self, e, semname, val):
        sems = self.sems

        def emit(eng):
            eng.wait_ge(sems[semname], val)
        self.ops[e].append(emit)

    def finalize(self, blk):
        prog = self

        def mk(e):
            def run(eng):
                for f in prog.ops[e]:
                    f(eng)
            return run
        blk.tensor(mk("t"))
        blk.vector(mk("v"))
        blk.scalar(mk("s"))
        blk.gpsimd(mk("g"))
        blk.sync(mk("y"))


def key(t, tag=""):
    return t.tensor.name if isinstance(t, bass.AP) else t.name


STREAM_RG = 512


def ap_keys(ap):
    if not isinstance(ap, bass.AP):
        return [ap]
    name = ap.tensor.name
    if (name.startswith("xt") or name.startswith("xb")) and name[2:].isdigit():
        rs = int(ap.tensor.shape[-1])
        off = int(ap.offset) % rs
        inner = ap.ap[-1]
        assert int(inner[0]) == 1
        lo, hi = off, off + int(inner[1])
        return [f"{name}@r{r}" for r in range(lo // STREAM_RG, (hi + STREAM_RG - 1) // STREAM_RG)]
    return [name]


def norm_keys(lst):
    out = []
    for x in lst:
        out.extend(ap_keys(x))
    return out


def w_matmul(P, out_ps, lhsT, rhs, start, stop, pskey, extra_reads=()):
    reads = [lhsT, rhs] + list(extra_reads)
    if start:
        writes = [pskey]
        tok = P.op("t", lambda t: t.matmul(out_ps, lhsT, rhs, start=start, stop=stop),
                   reads=reads, writes=writes)
    else:
        tok = P.op("t", lambda t: t.matmul(out_ps, lhsT, rhs, start=start, stop=stop),
                   reads=reads, writes=[])
        if stop:
            P.last_write[pskey] = tok
            P.readers[pskey] = []
    return tok


def w_transpose(P, out_ps, in_sb, ident, pskey):
    return P.op("t", lambda t: t.transpose(out_ps, in_sb, ident),
                reads=[in_sb, ident], writes=[pskey])


def w_copy(P, e, dst, src, dstkey=None, srckey=None):
    if e == "s":
        return P.op(e, lambda x: x.copy(dst, src),
                    reads=[srckey if srckey is not None else src],
                    writes=[dstkey if dstkey is not None else dst])
    return P.op(e, lambda x: x.tensor_copy(dst, src),
                reads=[srckey if srckey is not None else src],
                writes=[dstkey if dstkey is not None else dst])


def w_tt(P, e, op, dst, a, b, dkey=None, rkeys=None):
    return P.op(e, lambda x: x.tensor_tensor(dst, a, b, op),
                reads=rkeys if rkeys is not None else [a, b],
                writes=[dkey if dkey is not None else dst])


def w_ts(P, e, dst, a, s1, s2, op0, op1=ALU.bypass, dkey=None, rkeys=None):
    if s2 is None:
        return P.op(e, lambda x: x.tensor_single_scalar(dst, a, s1, op0),
                    reads=rkeys if rkeys is not None else [a] + ([s1] if isinstance(s1, bass.AP) else []),
                    writes=[dkey if dkey is not None else dst])
    return P.op(e, lambda x: x.tensor_scalar(dst, a, s1, s2, op0, op1),
                reads=rkeys if rkeys is not None else [a] + [s for s in (s1, s2) if isinstance(s, bass.AP)],
                writes=[dkey if dkey is not None else dst])


def w_stt(P, e, dst, a, scalar, b, op0, op1, dkey=None, rkeys=None):
    return P.op(e, lambda x: x.scalar_tensor_tensor(dst, a, scalar, b, op0, op1),
                reads=rkeys if rkeys is not None else [a, b] + ([scalar] if isinstance(scalar, bass.AP) else []),
                writes=[dkey if dkey is not None else dst])


def w_act(P, dst, src, func, bias=0.0, scale=1.0, accum=None, dkey=None, rkeys=None):
    writes = [dkey if dkey is not None else dst] + ([accum] if accum is not None else [])
    reads = rkeys if rkeys is not None else [src] + [x for x in (bias, scale) if isinstance(x, bass.AP)]
    return P.op("s", lambda s: s.activation(dst, src, func, bias=bias, scale=scale,
                                            accum_out=accum),
                reads=reads, writes=writes)


def w_reduce(P, e, dst, src, op, dkey=None, srckey=None, negate=None):
    return P.op(e, lambda x: x.tensor_reduce(dst, src, AX.X, op, negate=negate),
                reads=[srckey if srckey is not None else src],
                writes=[dkey if dkey is not None else dst])


def w_memset(P, e, dst, val, dkey=None):
    return P.op(e, lambda x: x.memset(dst, val),
                reads=[], writes=[dkey if dkey is not None else dst])


def w_recip(P, dst, src, dkey=None, srckey=None):
    return P.op("v", lambda v: v.reciprocal(dst, src),
                reads=[srckey if srckey is not None else src],
                writes=[dkey if dkey is not None else dst])


def w_dma(P, e, semname, dst, src, dstkey=None, srckey=None):
    if semname is None:
        semname = "dm_" + key(dst)
    return P.dma(e, semname, lambda x: x.dma_start(out=dst, in_=src),
                 reads=[srckey if srckey is not None else src],
                 writes=[dstkey if dstkey is not None else dst])


def w_dma_batch(P, e, pairs, dstkey, srckeys, semname=None):
    if semname is None:
        semname = "dm_" + dstkey
    fns = [(lambda x, d=d, s=s: x.dma_start(out=d, in_=s)) for d, s in pairs]
    return P.dma_batch(e, semname, fns, reads=srckeys, writes=[dstkey])


class Ctx:
    pass


def make_ctx(nc, es, P, need_inputs):
    c = Ctx()
    c.nc, c.es, c.P = nc, es, P
    c.inp = {}
    for nm, shp, dt in need_inputs:
        c.inp[nm] = nc.dram_tensor(nm, shp, dt, kind="ExternalInput").ap()
    c.out = nc.dram_tensor("out", [D, NTOK], F32, kind="ExternalOutput").ap()

    c._scopes = [es]

    def SB(n, s, d):
        return c._scopes[-1].enter_context(nc.sbuf_tensor(n, s, d))

    def PS(n, s, d):
        return c._scopes[-1].enter_context(nc.psum_tensor(n, s, d))

    def push_scope():
        c._scopes.append(ExitStack())

    def pop_scope():
        c.P.full_barrier()
        c._scopes.pop().close()
    c.SB, c.PS = SB, PS
    c.push_scope, c.pop_scope = push_scope, pop_scope
    c.xt = [SB(f"xt{i}", [128, NTOK], F32) for i in range(ND)]
    c.xb = [SB(f"xb{i}", [128, NTOK], BF16) for i in range(ND)]
    c.onec_f = SB("s_c_onec_f", [128, 1], F32)
    c.onec_bf = SB("s_c_onec_bf", [128, 1], BF16)
    c.one1_f = SB("s_c_one1_f", [1, 128], F32)
    c.one1_bf = SB("s_c_one1_bf", [1, 128], BF16)
    c.ps = [PS(f"psm{i}", [128, 512], F32) for i in range(4)]
    c.psx = [PS(f"psx{i}", [128, 512], F32) for i in range(2)]
    c.pst = PS("pst", [128, 1024], BF16)
    c.psr = PS("psr", [128, 512], F32)
    for t, nm in [(c.onec_f, "c_onec_f"), (c.one1_f, "c_one1_f"),
                  (c.one1_bf, "c_one1_bf"), (c.onec_bf, "c_onec_bf")]:
        w_dma(P, "y", None, t[:], c.inp[nm])
    return c


def load_const(c, sbuf_name, inp_name):
    ap = c.inp[inp_name]
    t = c.SB(sbuf_name, list(ap.shape), ap.dtype)
    w_dma(c.P, "y", None, t[:], ap)
    return t


def const_inputs_spec():
    return [("c_id_bf", [128, 128], BF16), ("c_id_f", [128, 128], F32),
            ("c_onec_f", [128, 1], F32), ("c_onec_bf", [128, 1], BF16),
            ("c_oner_f", [1, 512], F32),
            ("c_one1_f", [1, 128], F32), ("c_one1_bf", [1, 128], BF16),
            ("c_sl_f", [128, 128], F32), ("c_tu_f", [128, 128], F32),
            ("c_slneg", [128, 128], F32), ("c_suneg", [128, 128], F32),
            ("c_cm01", [128, 896], BF16)]


def const_inputs_host():
    import ml_dtypes
    BF = ml_dtypes.bfloat16
    idm = np.eye(128, dtype=np.float32)
    sl = np.tril(np.ones((128, 128), np.float32), -1)
    tu = np.triu(np.ones((128, 128), np.float32), 0)
    f = np.arange(896)[None, :]
    p = np.arange(128)[:, None]
    cm = (f >= p + 384).astype(np.float32)
    slneg = -np.tril(np.ones((128, 128), np.float32), -1)
    suneg = -np.triu(np.ones((128, 128), np.float32), 1)
    return {"c_id_bf": idm.astype(BF), "c_id_f": idm,
            "c_slneg": slneg, "c_suneg": suneg,
            "c_onec_f": np.ones((128, 1), np.float32),
            "c_onec_bf": np.ones((128, 1)).astype(BF),
            "c_oner_f": np.ones((1, 512), np.float32),
            "c_one1_f": np.ones((1, 128), np.float32),
            "c_one1_bf": np.ones((1, 128)).astype(BF),
            "c_sl_f": sl, "c_tu_f": tu, "c_cm01": cm.astype(BF)}


def emit_load_stream(c):
    P = c.P
    for i in range(ND):
        w_dma(P, "y", None, c.xt[i][:], c.inp["xin"][i * 128:(i + 1) * 128, :])
        w_copy(P, "v" if i % 2 == 0 else "s", c.xb[i][:], c.xt[i][:])


def emit_store_stream(c):
    P = c.P
    for i in range(ND):
        w_dma(P, "y", "so", c.out[i * 128:(i + 1) * 128, :], c.xt[i][:])
    P.barrier_wait("y", "so", 16 * P.dma_issued["so"])


def load_weight_bf16(c, dst_sb, dram_ap, rows, cols):
    w_dma(c.P, "g", None, dst_sb[:, 0:(rows // 128) * cols], dram_ap,
          dstkey=key(dst_sb), srckey=key(dram_ap))


def ln_update(c, res, chunk, scr, W=512, col0=None):
    P = c.P
    if col0 is None:
        col0 = chunk * W
    cs = slice(col0, col0 + W)
    rap = [r if isinstance(r, bass.AP) else r[:] for r in res]
    for d in range(ND):
        w_matmul(P, c.psr[0:1, 0:W], c.onec_f[:], rap[d], d == 0, d == ND - 1, "psr")
    w_copy(P, "v", scr["rowm"][:, 0:W], c.psr[0:1, 0:W], dstkey=key(scr["rowm"]), srckey="psr")
    for d in range(ND):
        w_act(P, scr["sq"][:, 0:W], rap[d], AF.Square, dkey=key(scr["sq"]), rkeys=[key(rap[d])])
        w_matmul(P, c.psr[0:1, 0:W], c.onec_bf[:], scr["sq"][:, 0:W], d == 0, d == ND - 1, "psr")
    w_copy(P, "v", scr["rowq"][:, 0:W], c.psr[0:1, 0:W], dstkey=key(scr["rowq"]), srckey="psr")
    w_ts(P, "v", scr["rowm"][:, 0:W], scr["rowm"][:, 0:W], 1.0 / D, None, ALU.mult,
         dkey=key(scr["rowm"]), rkeys=[key(scr["rowm"])])
    w_ts(P, "v", scr["rowq"][:, 0:W], scr["rowq"][:, 0:W], 1.0 / D, None, ALU.mult,
         dkey=key(scr["rowq"]), rkeys=[key(scr["rowq"])])
    w_tt(P, "v", ALU.mult, scr["rowa"][:, 0:W], scr["rowm"][:, 0:W], scr["rowm"][:, 0:W],
         dkey=key(scr["rowa"]), rkeys=[key(scr["rowm"])])
    w_tt(P, "v", ALU.subtract, scr["rowq"][:, 0:W], scr["rowq"][:, 0:W], scr["rowa"][:, 0:W],
         dkey=key(scr["rowq"]), rkeys=[key(scr["rowq"]), key(scr["rowa"])])
    w_ts(P, "v", scr["rowq"][:, 0:W], scr["rowq"][:, 0:W], 1e-5, None, ALU.add,
         dkey=key(scr["rowq"]), rkeys=[key(scr["rowq"])])
    P.op("s", lambda s: s.activation(scr["rowa"][:, 0:W], scr["rowq"][:, 0:W], AF.Sqrt),
         reads=[key(scr["rowq"])], writes=[key(scr["rowa"])])
    P.op("v", lambda v: v.reciprocal(scr["rowq"][:, 0:W], scr["rowa"][:, 0:W]),
         reads=[key(scr["rowa"])], writes=[key(scr["rowq"])])
    w_tt(P, "v", ALU.mult, scr["rowa"][:, 0:W], scr["rowm"][:, 0:W], scr["rowq"][:, 0:W],
         dkey=key(scr["rowa"]), rkeys=[key(scr["rowm"]), key(scr["rowq"])])
    w_ts(P, "v", scr["rowa"][:, 0:W], scr["rowa"][:, 0:W], -1.0, None, ALU.mult,
         dkey=key(scr["rowa"]), rkeys=[key(scr["rowa"])])
    w_matmul(P, c.psx[0][:, 0:W], c.one1_f[:], scr["rowq"][:, 0:W], True, True, "psx0")
    w_matmul(P, c.psx[1][:, 0:W], c.one1_f[:], scr["rowa"][:, 0:W], True, True, "psx1")
    for d in range(ND):
        w_tt(P, "v", ALU.mult, scr["t1"][:, 0:W], rap[d], c.psx[0][:, 0:W],
             dkey=key(scr["t1"]), rkeys=[key(rap[d]), "psx0"])
        P.op("v", lambda v, d=d: v.tensor_tensor(c.xt[d][:, cs], scr["t1"][:, 0:W],
             c.psx[1][:, 0:W], ALU.add), reads=[key(scr["t1"]), "psx1"],
             writes=[c.xt[d][:, cs]])
        P.op("g", lambda g, d=d: g.tensor_copy(c.xb[d][:, cs], c.xt[d][:, cs]),
             reads=[c.xt[d][:, cs]], writes=[c.xb[d][:, cs]])


def ln_scratch(c, pfx, t1=None, W=512):
    return {"sq": c.SB(pfx + "_sq", [128, W], BF16),
            "rowm": c.SB(pfx + "_rowm", [1, W], F32),
            "rowq": c.SB(pfx + "_rowq", [1, W], F32),
            "rowa": c.SB(pfx + "_rowa", [1, W], F32),
            "t1": t1 if t1 is not None else c.SB(pfx + "_t1", [128, W], F32)}


def emit_l1f(c):
    P = c.P
    SB = lambda n, s, d: c.es.enter_context(c.nc.sbuf_tensor(n, s, d))
    wr = SB("f_wr", [128, 8 * D], BF16)
    wk = SB("f_wk", [128, 8 * D], BF16)
    wv = SB("f_wv", [128, 8 * D], BF16)
    mur = SB("f_mur", [128, 8], F32)
    muk = SB("f_muk", [128, 8], F32)
    load_weight_bf16(c, wr, c.inp["wrT"], D, D)
    load_weight_bf16(c, wk, c.inp["wkT"], D, D)
    load_weight_bf16(c, wv, c.inp["wvT"], D, D)
    w_dma(P, "y", None, mur[:], c.inp["mur"])
    w_dma(P, "y", None, muk[:], c.inp["muk"])

    QT = NTOK // 512
    diff = SB("f_diff", [128, 512], F32)
    xr = [SB(f"f_xr{d}", [128, 512], BF16) for d in range(ND)]
    xk = [SB(f"f_xk{d}", [128, 512], BF16) for d in range(ND)]
    rbuf = [SB(f"f_r{d}", [128, 512], BF16) for d in range(ND)]
    krel = [SB("f_kr", [128, 512], BF16)] * ND
    kact = [SB(f"f_ka{d}", [128, 512], BF16) for d in range(ND)]
    res = [SB(f"f_res{d}", [128, 512], F32) for d in range(ND)]
    pcol = [SB(f"f_pc{d}", [128, 1], F32) for d in range(ND)]
    lsc = ln_scratch(c, "f_ln", t1=diff)

    for q in range(QT):
        t0 = q * 512
        for d in range(ND):
            w_tt(P, "v", ALU.subtract, diff[:, 1:512], c.xt[d][:, t0 + 1:t0 + 512],
                 c.xt[d][:, t0:t0 + 511], dkey=key(diff),
                 rkeys=[c.xt[d][:, t0:t0 + 512]])
            w_stt(P, "v", xr[d][:, 1:512], diff[:, 1:512], mur[:, d:d + 1],
                  c.xt[d][:, t0:t0 + 511], ALU.mult, ALU.add,
                  dkey=key(xr[d]), rkeys=[key(diff), key(mur), c.xt[d][:, t0:t0 + 512]])
            w_stt(P, "v", xk[d][:, 1:512], diff[:, 1:512], muk[:, d:d + 1],
                  c.xt[d][:, t0:t0 + 511], ALU.mult, ALU.add,
                  dkey=key(xk[d]), rkeys=[key(diff), key(muk), c.xt[d][:, t0:t0 + 512]])
            if q == 0:
                w_ts(P, "v", xr[d][:, 0:1], c.xt[d][:, 0:1], mur[:, d:d + 1], None,
                     ALU.mult, dkey=key(xr[d]), rkeys=[c.xt[d][:, 0:1], key(mur)])
                w_ts(P, "v", xk[d][:, 0:1], c.xt[d][:, 0:1], muk[:, d:d + 1], None,
                     ALU.mult, dkey=key(xk[d]), rkeys=[c.xt[d][:, 0:1], key(muk)])
            else:
                w_tt(P, "v", ALU.subtract, diff[:, 0:1], c.xt[d][:, t0:t0 + 1],
                     pcol[d][:], dkey=key(diff), rkeys=[c.xt[d][:, t0:t0 + 1], key(pcol[d])])
                w_stt(P, "v", xr[d][:, 0:1], diff[:, 0:1], mur[:, d:d + 1],
                      pcol[d][:], ALU.mult, ALU.add,
                      dkey=key(xr[d]), rkeys=[key(diff), key(mur), key(pcol[d])])
                w_stt(P, "v", xk[d][:, 0:1], diff[:, 0:1], muk[:, d:d + 1],
                      pcol[d][:], ALU.mult, ALU.add,
                      dkey=key(xk[d]), rkeys=[key(diff), key(muk), key(pcol[d])])
            w_copy(P, "v", pcol[d][:], c.xt[d][:, t0 + 511:t0 + 512],
                   srckey=c.xt[d][:, t0 + 511:t0 + 512])
        for n in range(ND):
            ps = c.ps[n % 2]
            pk = f"psm{n % 2}"
            for d in range(ND):
                w_matmul(P, ps[:, 0:512], wr[:, d * D + n * 128:d * D + n * 128 + 128],
                         xr[d][:], d == 0, d == ND - 1, pk)
            w_act(P, rbuf[n][:], ps[:, 0:512], AF.Sigmoid, rkeys=[pk])
            ps2 = c.ps[2 + n % 2]
            pk2 = f"psm{2 + n % 2}"
            for d in range(ND):
                w_matmul(P, ps2[:, 0:512], wk[:, d * D + n * 128:d * D + n * 128 + 128],
                         xk[d][:], d == 0, d == ND - 1, pk2)
            P.op("v", lambda v, n=n, ps2=ps2: v.tensor_scalar_max(krel[n][:], ps2[:, 0:512], 0.0),
                 reads=[pk2], writes=[key(krel[n])])
            P.op("s", lambda s, n=n: s.activation(kact[n][:], krel[n][:], AF.Square),
                 reads=[key(krel[n])], writes=[key(kact[n])])
        for n in range(ND):
            ps = c.ps[n % 2]
            pk = f"psm{n % 2}"
            for d in range(ND):
                w_matmul(P, ps[:, 0:512], wv[:, d * D + n * 128:d * D + n * 128 + 128],
                         kact[d][:], d == 0, d == ND - 1, pk)
            w_stt(P, "v", res[n][:], ps[:, 0:512], 1.0, rbuf[n][:], ALU.mult, ALU.mult,
                  dkey=key(res[n]), rkeys=[pk, key(rbuf[n])])
            w_tt(P, "v", ALU.add, res[n][:], res[n][:], c.xt[n][:, t0:t0 + 512],
                 rkeys=[key(res[n]), c.xt[n][:, t0:t0 + 512]])
        ln_update(c, res, q, lsc)


def emit_l1a(c):
    P = c.P
    SB = c.SB
    NH, DH = 16, 64
    wslot = SB("a_wslot", [128, 8 * D], BF16)
    V65 = [SB(f"a_v{t}", [128, 16 * 65], BF16) for t in range(NT)]
    KT = [SB(f"a_kt{j}", [128, NTOK], BF16) for j in range(8)]
    QTc = [SB(f"a_qt{j}", [128, 512], BF16) for j in range(8)]
    OTc = [SB(f"a_ot{j}", [128, 512], BF16) for j in range(8)]
    PP = [SB(f"a_p{i}", [128, 512], BF16) for i in range(8)]
    sq = PP[7]
    onec_bf = SB("a_onec_bf", [128, 1], BF16)
    negb = SB("a_negb", [128, 8], F32)
    qm = SB("a_qm", [1, 8], F32)
    km = SB("a_km", [1, 8], F32)
    s11 = SB("a_s11", [1, 1], F32)
    denb = SB("a_denb", [64, 512], BF16)
    denr_bf = SB("a_denr", [1, 512], BF16)
    res = [PP[d][:].bitcast(F32)[:, 0:128] for d in range(ND)]
    res_keys = [key(PP[d]) for d in range(ND)]
    lsc = ln_scratch(c, "a_ln", W=128)
    cm01 = load_const(c, "s_cm01", "c_cm01")
    P.op("v", lambda v: v.memset(onec_bf[:], 1.0), writes=[key(onec_bf)])

    load_weight_bf16(c, wslot, c.inp["avT"], D, D)
    for t in range(NT):
        P.op("v", lambda v, t=t: v.memset(
            V65[t][:].rearrange("p (h c) -> p h c", c=65)[:, :, 64:65], 1.0),
            writes=[key(V65[t])])
        for nch in range(2):
            ps = c.ps[nch % 2]
            pk = f"psm{nch % 2}"
            for d in range(ND):
                w_matmul(P, ps[:, 0:512], c.xb[d][:, t * 128:(t + 1) * 128],
                         wslot[:, d * D + nch * 512:d * D + nch * 512 + 512],
                         d == 0, d == ND - 1, pk)
            for hh in range(8):
                h = nch * 8 + hh
                w_copy(P, "v" if hh % 2 == 0 else "s",
                       V65[t][:, h * 65:h * 65 + 64], ps[:, hh * 64:hh * 64 + 64],
                       dstkey=key(V65[t]), srckey=pk)

    load_weight_bf16(c, wslot, c.inp["akT"], D, D)
    for j in range(8):
        for ch in range(4):
            ps = c.ps[ch % 2]
            pk = f"psm{ch % 2}"
            for d in range(ND):
                w_matmul(P, ps[:, 0:512],
                         wslot[:, d * D + j * 128:d * D + j * 128 + 128],
                         c.xb[d][:, ch * 512:ch * 512 + 512],
                         d == 0, d == ND - 1, pk)
            w_copy(P, "v" if ch % 2 == 0 else "s", KT[j][:, ch * 512:ch * 512 + 512],
                   ps[:, 0:512], dstkey=key(KT[j]), srckey=pk)
        for ch in range(4):
            w_act(P, sq[:], KT[j][:, ch * 512:ch * 512 + 512], AF.Square,
                  dkey=key(sq), rkeys=[key(KT[j])])
            w_matmul(P, c.psr[0:1, 0:512], onec_bf[:], sq[:], True, True, "psr")
            w_reduce(P, "v", s11[:], c.psr[0:1, 0:512], ALU.max, srckey="psr")
            if ch == 0:
                w_copy(P, "v", km[:, j:j + 1], s11[:], dstkey=key(km), srckey=key(s11))
            else:
                P.op("v", lambda v, j=j: v.tensor_tensor(
                    km[:, j:j + 1], km[:, j:j + 1], s11[:], ALU.max),
                    reads=[key(km), key(s11)], writes=[key(km)])

    for qc in range(4):
        load_weight_bf16(c, wslot, c.inp["aqT"], D, D)
        for j in range(8):
            ps = c.ps[j % 2]
            pk = f"psm{j % 2}"
            for d in range(ND):
                w_matmul(P, ps[:, 0:512],
                         wslot[:, d * D + j * 128:d * D + j * 128 + 128],
                         c.xb[d][:, qc * 512:qc * 512 + 512],
                         d == 0, d == ND - 1, pk)
            w_copy(P, "v" if j % 2 == 0 else "s", QTc[j][:], ps[:, 0:512], srckey=pk)
            w_act(P, sq[:], QTc[j][:], AF.Square, dkey=key(sq), rkeys=[key(QTc[j])])
            w_matmul(P, c.psr[0:1, 0:512], onec_bf[:], sq[:], True, True, "psr")
            w_reduce(P, "v", s11[:], c.psr[0:1, 0:512], ALU.max, srckey="psr")
            w_tt(P, "v", ALU.add, s11[:], s11[:], km[:, j:j + 1],
                 dkey=key(s11), rkeys=[key(s11), key(km)])
            w_ts(P, "v", s11[:], s11[:], -1.0 / 16.0, None, ALU.mult)
            w_matmul(P, c.psx[0][:, 0:1], c.one1_f[:], s11[:], True, True, "psx0")
            w_copy(P, "v", negb[:, j:j + 1], c.psx[0][:, 0:1], dstkey=key(negb), srckey="psx0")

        load_weight_bf16(c, wslot, c.inp["aoT"], D, D)

        AVB = [(c.psx[0], "psx0"), (c.ps[0], "psm0"), (c.psx[1], "psx1"), (c.ps[1], "psm1")]

        def denscale(h):
            j = h // 2
            oh = (h % 2) * 64
            pav, pkav = AVB[h % 4]
            P.op("v", lambda v, pav=pav: v.reciprocal(c.psr[64:65, 0:512],
                 pav[64:65, 0:512]), reads=[pkav], writes=["psr"])
            P.op("v", lambda v: v.tensor_copy(denr_bf[:], c.psr[64:65, 0:512]),
                 reads=["psr"], writes=[key(denr_bf)])
            w_matmul(P, c.psr[0:64, 0:512], c.one1_bf[:, 0:64], denr_bf[:], True, True, "psr")
            w_copy(P, "s", denb[:], c.psr[0:64, 0:512], srckey="psr")
            w_stt(P, "v", OTc[j][oh:oh + 64, :],
                  pav[0:64, 0:512], 1.0, denb[:], ALU.mult, ALU.mult,
                  dkey=key(OTc[j]), rkeys=[pkav, key(denb)])

        kbmax = qc * 4 + 3
        for hp in range(8):
            h0, h1 = 2 * hp, 2 * hp + 1
            j = hp // 1 // 2 * 0 + hp // 2
            for kb in range(kbmax + 1):
                for hi, h in enumerate((h0, h1)):
                    jt = h // 2
                    oh = (h % 2) * 64
                    pst_b = c.ps[2 + hi]
                    pkst = f"psm{2 + hi}"
                    w_matmul(P, pst_b[:, 0:512],
                             KT[jt][oh:oh + 64, kb * 128:kb * 128 + 128],
                             QTc[jt][oh:oh + 64, :],
                             True, True, pkst)
                    pt = PP[4 * hi + kb % 4]
                    w_act(P, pt[:], pst_b[:, 0:512], AF.Exp,
                          bias=negb[:, jt:jt + 1], scale=0.125,
                          dkey=key(pt), rkeys=[pkst, key(negb)])
                    if kb >= qc * 4:
                        v_ = kb - qc * 4
                        w_tt(P, "v", ALU.mult, pt[:], pt[:],
                             cm01[:, 384 - 128 * v_:896 - 128 * v_],
                             rkeys=[key(pt), key(cm01)])
                    pav, pkav = AVB[h % 4]
                    w_matmul(P, pav[0:65, 0:512],
                             V65[kb][:, h * 65:h * 65 + 65], pt[:],
                             kb == 0, kb == kbmax, pkav)
            denscale(h0)
            denscale(h1)

        for sub in range(4):
            col0 = qc * 512 + sub * 128
            for d in range(ND):
                ps = c.ps[d % 2]
                pk = f"psm{d % 2}"
                for n in range(ND):
                    w_matmul(P, ps[:, 0:128],
                             wslot[:, n * D + d * 128:n * D + d * 128 + 128],
                             OTc[n][:, sub * 128:sub * 128 + 128],
                             n == 0, n == ND - 1, pk)
                P.op("v", lambda v, d=d, ps=ps, col0=col0: v.tensor_tensor(
                    res[d], ps[:, 0:128], c.xt[d][:, col0:col0 + 128], ALU.add),
                    reads=[pk, c.xt[d][:, col0:col0 + 128]], writes=[res_keys[d]])
            ln_update(c, res, None, lsc, W=128, col0=col0)


def emit_l0a(c):
    P = c.P
    SB = c.SB
    H, DK = 4, 256
    NC = 16
    id_bf = load_const(c, "s_id_bf", "c_id_bf")
    sl_f = load_const(c, "s_sl", "c_sl_f")
    tu_f = load_const(c, "s_tu", "c_tu_f")
    gn_row = SB("d_gnrow", [1, DK], F32)
    w_dma(P, "y", None, gn_row[:], c.inp["dgn"])
    gnb = SB("d_gnb", [128, DK], F32)
    w_matmul(P, c.psx[0][:, 0:DK], c.one1_f[:], gn_row[:], True, True, "psx0")
    w_copy(P, "v", gnb[:], c.psx[0][:, 0:DK], srckey="psx0")
    bslot = SB("d_bslot", [128, 32], F32)
    w_dma(P, "y", None, bslot[:], c.inp["dwbeta"])
    beta = SB("d_beta", [128, 64], F32)
    negbeta = SB("d_negbeta", [128, 64], F32)
    for t in range(NT):
        for d in range(ND):
            w_matmul(P, c.psr[:, 0:4], c.xt[d][:, t * 128:(t + 1) * 128],
                     bslot[:, d * 4:(d + 1) * 4], d == 0, d == ND - 1, "psr")
        w_act(P, beta[:, t * 4:(t + 1) * 4], c.psr[:, 0:4], AF.Sigmoid,
              dkey=key(beta), rkeys=["psr"])
    w_ts(P, "v", negbeta[:], beta[:], -1.0, None, ALU.mult,
         dkey=key(negbeta), rkeys=[key(beta)])

    OT = [SB(f"d_ot{j}", [128, NTOK], BF16) for j in range(8)]

    for pr in range(2):
        c.push_scope()
        QT = [[SB(f"d_qt{pr}_{hh}_{j}", [128, NTOK], BF16) for j in range(2)] for hh in range(2)]
        KT = [[SB(f"d_kt{pr}_{hh}_{j}", [128, NTOK], BF16) for j in range(2)] for hh in range(2)]
        ktok = [SB(f"d_ktok{pr}_{t}", [128, 2 * DK], BF16) for t in range(NT)]
        vtok = [SB(f"d_vtok{pr}_{t}", [128, 2 * DK], BF16) for t in range(NT)]
        nsq = SB(f"d_nsq{pr}", [128, 1], F32)

        bankP = [c.ps[1], c.ps[3]]
        bankX = [c.ps[0], c.ps[2]]
        kPa = ["pP0", "pP1"]
        kPb = ["pP0", "pP1"]
        kXa = ["pX0", "pX1"]
        kXb = ["pX0", "pX1"]

        c.push_scope()
        qslot = SB(f"d_qslot{pr}", [128, 8 * DK], BF16)
        kslot = SB(f"d_kslot{pr}", [128, 8 * DK], BF16)
        vslot = qslot
        qn = SB(f"d_qn{pr}", [128, DK], BF16)
        sqd_t = SB(f"d_sqd{pr}", [128, DK], F32)
        sqd = sqd_t[:]
        for hh in range(2):
            h = 2 * pr + hh
            for slot, wn in ((qslot, "dwq"), (kslot, "dwk")):
                w_dma(P, "g", None, slot[:], c.inp[wn][h * 128:(h + 1) * 128, :],
                      dstkey=key(slot), srckey=key(c.inp[wn]))
            for which, slot in (("q", qslot), ("k", kslot), ("v", vslot)):
                if which == "v":
                    w_dma(P, "g", None, vslot[:], c.inp["dwv"][h * 128:(h + 1) * 128, :],
                          dstkey=key(vslot), srckey=key(c.inp["dwv"]))
                for t in range(NT):
                    ps = c.ps[t % 4]
                    pk = f"psm{t % 4}"
                    for d in range(ND):
                        w_matmul(P, ps[:, 0:DK], c.xb[d][:, t * 128:(t + 1) * 128],
                                 slot[:, d * DK:(d + 1) * DK], d == 0, d == ND - 1, pk)
                    if which == "v":
                        P.op("v", lambda v, t=t, hh=hh, ps=ps, h=h, vtok=vtok, beta=beta:
                            v.tensor_single_scalar(
                            vtok[t][:, hh * DK:(hh + 1) * DK], ps[:, 0:DK],
                            beta[:, t * 4 + h:t * 4 + h + 1], ALU.mult),
                            reads=[pk, key(beta)], writes=[key(vtok[t])])
                        continue
                    w_act(P, sqd, ps[:, 0:DK], AF.Square, accum=nsq[:],
                          dkey=key(sqd_t), rkeys=[pk])
                    P.op("v", lambda v, nsq=nsq: v.reciprocal(nsq[:], nsq[:]),
                         reads=[key(nsq)], writes=[key(nsq)])
                    P.op("s", lambda s, nsq=nsq: s.activation(nsq[:], nsq[:], AF.Sqrt),
                         reads=[key(nsq)], writes=[key(nsq)])
                    dst_tok = qn if which == "q" else ktok[t]
                    dsl = dst_tok[:] if which == "q" else dst_tok[:, hh * DK:(hh + 1) * DK]
                    P.op("v", lambda v, dsl=dsl, ps=ps, nsq=nsq, pk=pk: v.tensor_single_scalar(
                        dsl, ps[:, 0:DK], nsq[:], ALU.mult),
                        reads=[pk, key(nsq)], writes=[key(dst_tok)])
                    dstT = QT[hh] if which == "q" else KT[hh]
                    for j in range(2):
                        tb = (2 * t + j) % 3
                        if tb == 0:
                            tps, tkey = c.pst[:, j * 128:j * 128 + 128], "pst"
                        else:
                            tps = c.psx[tb - 1][:].bitcast(BF16)[:, 0:128]
                            tkey = f"psx{tb - 1}"
                        src_ap = (qn[:, j * 128:(j + 1) * 128] if which == "q"
                                  else ktok[t][:, hh * DK + j * 128:hh * DK + (j + 1) * 128])
                        w_transpose(P, tps, src_ap, id_bf[:], tkey)
                        w_copy(P, "s", dstT[j][:, t * 128:(t + 1) * 128], tps,
                               dstkey=key(dstT[j]), srckey=tkey)
        c.pop_scope()
        S32 = [[SB(f"d_s32_{pr}_{hh}_{j}", [128, DK], F32) for j in range(2)] for hh in range(2)]
        Sbf = [[SB(f"d_sbf_{pr}_{hh}_{j}", [128, DK], BF16) for j in range(2)] for hh in range(2)]
        tmpf0 = SB(f"d_tmpf{pr}", [128, 256], F32)
        tmpf = [tmpf0, tmpf0]
        Pbf = [SB(f"d_pbf{pr}_{hh}", [128, 128], BF16) for hh in range(2)]
        PTbf = [SB(f"d_ptbf{pr}_{hh}", [128, 128], BF16) for hh in range(2)]
        TTbf = [SB(f"d_ttbf{pr}_{hh}", [128, 128], BF16) for hh in range(2)]
        Xbf = [SB(f"d_xbf{pr}_{hh}", [128, DK], BF16) for hh in range(2)]
        oto = [SB(f"d_oto{pr}_{hh}", [128, DK], BF16) for hh in range(2)]
        sqd = tmpf0[:, 0:256]


        def prefetch_MK(ci2):
            bank = c.psx[ci2 % 2]
            bkey = f"psx{ci2 % 2}"
            cs2 = slice(ci2 * 128, ci2 * 128 + 128)
            for hh in range(2):
                off = 256 * hh
                for j in range(2):
                    w_matmul(P, bank[:, off:off + 128], KT[hh][j][:, cs2],
                             KT[hh][j][:, cs2], j == 0, j == 1, bkey)
                for j in range(2):
                    w_matmul(P, bank[:, off + 128:off + 256], KT[hh][j][:, cs2],
                             QT[hh][j][:, cs2], j == 0, j == 1, bkey)
        prefetch_MK(0)
        for ci in range(NC):
            cs = slice(ci * 128, ci * 128 + 128)
            first = ci == 0
            mkbank = c.psx[ci % 2]
            mkkey = f"psx{ci % 2}"
            for hh in range(2):
                h = 2 * pr + hh
                bcol = beta[:, ci * 4 + h:ci * 4 + h + 1]
                if not first:
                    for j in range(2):
                        w_copy(P, "v", Sbf[hh][j][:], S32[hh][j][:])
                off = 256 * hh
                w_tt(P, "v", ALU.mult, TTbf[hh][:], mkbank[:, off + 128:off + 256], tu_f[:],
                     dkey=key(TTbf[hh]), rkeys=[mkkey, key(tu_f)])
                w_stt(P, "v", Pbf[hh][:], mkbank[:, off:off + 128],
                      negbeta[:, ci * 4 + h:ci * 4 + h + 1], sl_f[:],
                      ALU.mult, ALU.mult,
                      dkey=key(Pbf[hh]), rkeys=[mkkey, key(negbeta), key(sl_f)])
                w_transpose(P, c.pst[:, (4 + hh) * 128:(5 + hh) * 128], Pbf[hh][:],
                            id_bf[:], "pst")
                w_copy(P, "v", PTbf[hh][:], c.pst[:, (4 + hh) * 128:(5 + hh) * 128],
                       dstkey=key(PTbf[hh]), srckey="pst")
                if first:
                    P.op("v", lambda v, hh=hh, ci=ci, Xbf=Xbf, vtok=vtok:
                        v.tensor_copy(Xbf[hh][:], vtok[ci][:, hh * DK:(hh + 1) * DK]),
                        reads=[key(vtok[ci])], writes=[key(Xbf[hh])])
                else:
                    for j in range(2):
                        w_matmul(P, bankX[hh][:, 0:DK], KT[hh][j][:, cs], Sbf[hh][j][:],
                                 j == 0, j == 1, kXa[hh])
                    P.op("v", lambda v, hh=hh, ci=ci, h=h, Xbf=Xbf, vtok=vtok, bankX=bankX,
                        negbeta=negbeta: v.scalar_tensor_tensor(
                        Xbf[hh][:], bankX[hh][:, 0:DK],
                        negbeta[:, ci * 4 + h:ci * 4 + h + 1],
                        vtok[ci][:, hh * DK:(hh + 1) * DK], ALU.mult, ALU.add),
                        reads=[kXa[hh], key(negbeta), key(vtok[ci])], writes=[key(Xbf[hh])])
            for lev in range(7):
                for hh in range(2):
                    w_matmul(P, bankX[hh][:, 0:DK], id_bf[:], Xbf[hh][:], True, False, kXa[hh])
                    w_matmul(P, bankX[hh][:, 0:DK], PTbf[hh][:], Xbf[hh][:], False, True, kXa[hh])
                for hh in range(2):
                    w_copy(P, "v", Xbf[hh][:], bankX[hh][:, 0:DK], srckey=kXa[hh])
                    if lev < 6:
                        w_matmul(P, bankP[hh][:, 0:128], PTbf[hh][:], Pbf[hh][:],
                                 True, True, kPa[hh])
                        w_matmul(P, bankP[hh][:, 256:384], Pbf[hh][:], PTbf[hh][:],
                                 True, True, kPb[hh])
                for hh in range(2):
                    if lev < 6:
                        w_copy(P, "s", Pbf[hh][:], bankP[hh][:, 0:128], srckey=kPa[hh])
                        w_copy(P, "s", PTbf[hh][:], bankP[hh][:, 256:384], srckey=kPb[hh])
                if lev == 2 and ci + 1 < NC:
                    prefetch_MK(ci + 1)
            for hh in range(2):
                for j in range(2):
                    kk = kPa[hh] if j == 0 else kPb[hh]
                    off = 0 if j == 0 else 256
                    w_matmul(P, bankP[hh][:, off:off + DK],
                             ktok[ci][:, hh * DK + j * 128:hh * DK + (j + 1) * 128],
                             Xbf[hh][:], True, True, kk)
                    if first:
                        w_copy(P, "v", S32[hh][j][:], bankP[hh][:, off:off + DK], srckey=kk)
                    else:
                        P.op("v", lambda v, hh=hh, j=j, off=off, S32=S32, bankP=bankP:
                            v.tensor_tensor(
                            S32[hh][j][:], S32[hh][j][:], bankP[hh][:, off:off + DK],
                            ALU.add), reads=[key(S32[hh][j]), kk], writes=[key(S32[hh][j])])
            for hh in range(2):
                h = 2 * pr + hh
                obank = c.psr if hh == 0 else bankP[1]
                obkey = "psr" if hh == 0 else kPa[1]
                osl = obank[:, 0:DK] if hh == 0 else obank[:, 256:512]
                if first:
                    w_matmul(P, osl, TTbf[hh][:], Xbf[hh][:], True, True, obkey)
                else:
                    for j in range(2):
                        w_matmul(P, osl, QT[hh][j][:, cs], Sbf[hh][j][:],
                                 j == 0, False, obkey)
                    w_matmul(P, osl, TTbf[hh][:], Xbf[hh][:], False, True, obkey)
                w_act(P, sqd, osl, AF.Square, accum=nsq[:],
                      dkey=key(tmpf[0]), rkeys=[obkey])
                w_ts(P, "v", nsq[:], nsq[:], 1.0 / DK, 1e-5, ALU.mult, ALU.add,
                     dkey=key(nsq), rkeys=[key(nsq)])
                P.op("v", lambda v, nsq=nsq: v.reciprocal(nsq[:], nsq[:]),
                     reads=[key(nsq)], writes=[key(nsq)])
                P.op("s", lambda s, nsq=nsq: s.activation(nsq[:], nsq[:], AF.Sqrt),
                     reads=[key(nsq)], writes=[key(nsq)])
                w_stt(P, "v", oto[hh][:], osl, nsq[:], gnb[:],
                      ALU.mult, ALU.mult, dkey=key(oto[hh]),
                      rkeys=[obkey, key(nsq), key(gnb)])
                for j in range(2):
                    if j == 0:
                        tps = c.pst[:, (2 * hh) * 128:(2 * hh) * 128 + 128]
                        tkey = "pst"
                    else:
                        tps = c.psx[ci % 2][:].bitcast(BF16)[:, hh * 128:hh * 128 + 128]
                        tkey = f"psx{ci % 2}"
                    w_transpose(P, tps, oto[hh][:, j * 128:(j + 1) * 128], id_bf[:], tkey)
                    w_copy(P, "s", OT[2 * h + j][:, cs], tps,
                           dstkey=key(OT[2 * h + j]), srckey=tkey)
        c.pop_scope()

    wo = SB("d_wo", [128, 8 * D], BF16)
    load_weight_bf16(c, wo, c.inp["dwo"], D, D)
    res = [SB(f"d_res{d}", [128, 512], F32) for d in range(ND)]
    lsc = ln_scratch(c, "d_ln", W=512)
    for ch in range(4):
        for d in range(ND):
            ps = c.ps[d % 2]
            pk = f"psm{d % 2}"
            for n in range(ND):
                w_matmul(P, ps[:, 0:512],
                         wo[:, n * D + d * 128:n * D + d * 128 + 128],
                         OT[n][:, ch * 512:ch * 512 + 512],
                         n == 0, n == ND - 1, pk)
            P.op("v", lambda v, d=d, ps=ps, ch=ch: v.tensor_tensor(
                res[d][:], ps[:, 0:512], c.xt[d][:, ch * 512:ch * 512 + 512], ALU.add),
                reads=[pk, c.xt[d][:, ch * 512:ch * 512 + 512]], writes=[key(res[d])])
        ln_update(c, res, None, lsc, W=512, col0=ch * 512)


L0F_NQ = 4


def emit_l0f(c):
    P = c.P
    SB = c.SB
    E = 8
    id_f = load_const(c, "s_id_f", "c_id_f")
    oner_f = load_const(c, "s_oner_f", "c_oner_f")
    slotA = SB("m_slotA", [128, 8 * D], BF16)
    slotB = SB("m_slotB", [128, 8 * D], BF16)
    zh = [SB(f"m_zh{i}", [128, 512], BF16) for i in range(16)]
    g = [SB(f"m_g{i}", [128, 512], BF16) for i in range(8)]
    xw = [SB(f"m_xw{i}", [128, 512], BF16) for i in range(8)]
    yq = [SB(f"m_y{i}", [128, 512], F32) for i in range(8)]
    gws = SB("m_gws", [128, 64], F32)
    b1c = SB("m_b1c", [128, 64], F32)
    wselq = SB("m_wsel", [128, 32], F32)
    keepq = SB("m_keep", [128, 32], F32)
    pcol = SB("m_pcol", [128, 8], F32)
    m1c = SB("m_m1c", [128, 1], F32)
    rsc = SB("m_rsc", [128, 1], F32)
    wrow = SB("m_wrow", [1, 512], F32)
    b2c = SB("m_b2c", [128, 64], F32)
    b3s = SB("m_b3s", [1, D], BF16)
    mrow_bf = SB("m_mrowbf", [1, 512], BF16)
    x3m = [SB(f"m_x3m{d}", [128, 512], BF16) for d in range(ND)]
    prod = SB("m_prod", [128, 512], F32)
    lsc = ln_scratch(c, "m_ln", W=512)
    w_dma(P, "y", None, gws[:], c.inp["gwT"])
    w_dma(P, "y", None, b1c[:], c.inp["b1c"])
    w_dma(P, "y", None, b2c[:], c.inp["b2c"])

    w1T, w2T, w3T = c.inp["w1T"], c.inp["w2T"], c.inp["w3T"]
    sw1T, sw2T, sw3T = c.inp["sw1T"], c.inp["sw2T"], c.inp["sw3T"]

    def load_slot(slot, dram, blk):
        w_dma(P, "g", None, slot[:], dram[blk * 128:(blk + 1) * 128, :],
              dstkey=key(slot), srckey=key(dram))

    for tq in range(L0F_NQ):
        t0 = tq * 512
        for tt in range(4):
            ti = tq * 4 + tt
            pr = c.psr
            for d in range(ND):
                w_matmul(P, pr[:, 0:8], c.xt[d][:, ti * 128:(ti + 1) * 128],
                         gws[:, d * 8:(d + 1) * 8], d == 0, d == ND - 1, "psr")
            w_reduce(P, "v", m1c[:], pr[:, 0:8], ALU.max, srckey="psr", negate=True)
            w_act(P, pcol[:], pr[:, 0:8], AF.Exp, bias=m1c[:], accum=rsc[:],
                  dkey=key(pcol), rkeys=["psr", key(m1c)])
            P.op("v", lambda v: v.reciprocal(rsc[:], rsc[:]),
                 reads=[key(rsc)], writes=[key(rsc)])
            w_ts(P, "v", pcol[:], pcol[:], rsc[:], None, ALU.mult,
                 dkey=key(pcol), rkeys=[key(pcol), key(rsc)])
            w_reduce(P, "v", m1c[:], pcol[:], ALU.max, srckey=key(pcol), dkey=key(m1c))
            w_ts(P, "v", keepq[:, tt * 8:(tt + 1) * 8], pcol[:], m1c[:], None,
                 ALU.is_ge, dkey=key(keepq), rkeys=[key(pcol), key(m1c)])
            w_stt(P, "v", wselq[:, tt * 8:(tt + 1) * 8], keepq[:, tt * 8:(tt + 1) * 8],
                  -1e30, pcol[:], ALU.mult, ALU.add,
                  dkey=key(wselq), rkeys=[key(keepq), key(pcol)])
            w_reduce(P, "v", m1c[:], wselq[:, tt * 8:(tt + 1) * 8], ALU.max,
                     srckey=key(wselq), dkey=key(m1c))
            w_ts(P, "v", keepq[:, tt * 8:(tt + 1) * 8], pcol[:], m1c[:], None,
                 ALU.is_ge, dkey=key(keepq), rkeys=[key(pcol), key(m1c)])
            w_tt(P, "v", ALU.mult, wselq[:, tt * 8:(tt + 1) * 8], pcol[:],
                 keepq[:, tt * 8:(tt + 1) * 8],
                 dkey=key(wselq), rkeys=[key(pcol), key(keepq)])

        for ih in range(2):
            load_slot(slotA, sw1T, ih)
            load_slot(slotB, sw2T, ih)
            for i8 in range(8):
                i = ih * 8 + i8
                ps = c.ps[i8 % 2]
                pk = f"psm{i8 % 2}"
                for d in range(ND):
                    w_matmul(P, ps[:, 0:512],
                             slotA[:, d * 1024 + i8 * 128:d * 1024 + i8 * 128 + 128],
                             c.xb[d][:, t0:t0 + 512], d == 0, d == ND - 1, pk)
                w_act(P, g[i8][:], ps[:, 0:512], AF.Silu, dkey=key(g[i8]), rkeys=[pk])
            for i8 in range(8):
                i = ih * 8 + i8
                ps = c.ps[i8 % 2]
                pk = f"psm{i8 % 2}"
                for d in range(ND):
                    w_matmul(P, ps[:, 0:512],
                             slotB[:, d * 1024 + i8 * 128:d * 1024 + i8 * 128 + 128],
                             c.xb[d][:, t0:t0 + 512], d == 0, d == ND - 1, pk)
                w_stt(P, "v", zh[i][:], ps[:, 0:512], 1.0, g[i8][:], ALU.mult, ALU.mult,
                      dkey=key(zh[i]), rkeys=[pk, key(g[i8])])
            if ih == 0:
                load_slot(slotA, sw1T, 1)
        load_slot(slotA, sw3T, 0)
        load_slot(slotB, sw3T, 1)
        for d in range(ND):
            ps = c.ps[d % 2]
            pk = f"psm{d % 2}"
            for i in range(8):
                w_matmul(P, ps[:, 0:512],
                         slotA[:, i * 1024 + d * 128:i * 1024 + d * 128 + 128],
                         zh[i][:], i == 0, False, pk)
            for i in range(8):
                w_matmul(P, ps[:, 0:512],
                         slotB[:, i * 1024 + d * 128:i * 1024 + d * 128 + 128],
                         zh[8 + i][:], False, i == 7, pk)
            w_copy(P, "v" if d % 2 == 0 else "s", yq[d][:], ps[:, 0:512], srckey=pk)

        load_slot(slotA, w1T, 0)
        load_slot(slotB, w3T, 0)
        for e in range(E):
            sl1, sl3 = slotA, slotB
            for tt in range(4):
                w_matmul(P, c.psr[0:1, 0:128], wselq[:, tt * 8 + e:tt * 8 + e + 1],
                         id_f[:], True, True, "psr")
                w_copy(P, "v", wrow[0:1, tt * 128:(tt + 1) * 128], c.psr[0:1, 0:128],
                       dstkey=key(wrow), srckey="psr")
                w_matmul(P, c.psr[0:1, 0:128], keepq[:, tt * 8 + e:tt * 8 + e + 1],
                         id_f[:], True, True, "psr")
                w_copy(P, "v", mrow_bf[0:1, tt * 128:(tt + 1) * 128], c.psr[0:1, 0:128],
                       dstkey=key(mrow_bf), srckey="psr")
            w_dma(P, "g", None, b3s[:], c.inp["b3r"][e:e + 1, :])
            w_matmul(P, c.psx[0][:, 0:512], c.one1_f[:], wrow[0:1, 0:512], True, True, "psx0")
            for d in range(ND):
                w_tt(P, "v", ALU.mult, xw[d][:], c.xt[d][:, t0:t0 + 512],
                     c.psx[0][:, 0:512], dkey=key(xw[d]),
                     rkeys=[c.xt[d][:, t0:t0 + 512], "psx0"])
            for d in range(ND):
                ps3 = c.ps[2 + d % 2]
                pk3 = f"psm{2 + d % 2}"
                for dd in range(ND):
                    w_matmul(P, ps3[:, 0:512],
                             sl3[:, dd * 1024 + d * 128:dd * 1024 + d * 128 + 128],
                             xw[dd][:], dd == 0, False, pk3)
                w_matmul(P, ps3[:, 0:512], b3s[0:1, d * 128:(d + 1) * 128],
                         mrow_bf[0:1, 0:512], False, True, pk3)
                w_copy(P, "s", x3m[d][:], ps3[:, 0:512], srckey=pk3)
            load_slot(sl3, w2T, e)
            for h in range(ND):
                ps = c.ps[h % 2]
                pk = f"psm{h % 2}"
                for d in range(ND):
                    w_matmul(P, ps[:, 0:512],
                             sl1[:, d * 1024 + h * 128:d * 1024 + h * 128 + 128],
                             xw[d][:], d == 0, d == ND - 1, pk)
                w_act(P, g[h][:], ps[:, 0:512], AF.Silu, bias=b1c[:, e * 8 + h:e * 8 + h + 1],
                      dkey=key(g[h]), rkeys=[pk, key(b1c)])
            if e + 1 < E:
                load_slot(sl1, w1T, e + 1)
            for d in range(ND):
                ps2 = c.ps[d % 2]
                pk2 = f"psm{d % 2}"
                for h in range(ND):
                    w_matmul(P, ps2[:, 0:512],
                             sl3[:, h * 1024 + d * 128:h * 1024 + d * 128 + 128],
                             g[h][:], h == 0, h == ND - 1, pk2)
                w_stt(P, "v", prod[:], ps2[:, 0:512], b2c[:, e * 8 + d:e * 8 + d + 1],
                      x3m[d][:], ALU.add, ALU.mult,
                      dkey=key(prod), rkeys=[pk2, key(b2c), key(x3m[d])])
                P.op("v", lambda v, d=d: v.tensor_tensor(yq[d][:], yq[d][:], prod[:],
                     ALU.add), reads=[key(yq[d]), key(prod)], writes=[key(yq[d])])
            if e + 1 < E:
                load_slot(slotB, w3T, e + 1)

        for d in range(ND):
            P.op("g", lambda gp, d=d, t0=t0: gp.tensor_tensor(yq[d][:], yq[d][:],
                 c.xt[d][:, t0:t0 + 512], ALU.add),
                 reads=[key(yq[d]), c.xt[d][:, t0:t0 + 512]], writes=[key(yq[d])])
        ln_update(c, [y[:] for y in yq], None, lsc, W=512, col0=t0)


STAGE_INPUTS = {
    "l1f": [("wrT", [128, 8 * D], F32), ("wkT", [128, 8 * D], F32), ("wvT", [128, 8 * D], F32),
            ("mur", [128, 8], F32), ("muk", [128, 8], F32)],
    "l1a": [("aqT", [128, 8 * D], F32), ("akT", [128, 8 * D], F32), ("avT", [128, 8 * D], F32),
            ("aoT", [128, 8 * D], F32)],
    "l0a": [("dwq", [512, 8 * 256], F32), ("dwk", [512, 8 * 256], F32), ("dwv", [512, 8 * 256], F32),
            ("dwbeta", [128, 32], F32), ("dwo", [128, 8 * D], F32), ("dgn", [1, 256], F32)],
    "l0f": [("gwT", [128, 64], F32), ("b1c", [128, 64], F32), ("b2c", [128, 64], F32),
            ("b3r", [8, D], F32),
            ("w1T", [8 * 128, 8 * D], F32), ("w2T", [8 * 128, 8 * D], F32),
            ("w3T", [8 * 128, 8 * D], F32),
            ("sw1T", [2 * 128, 8 * D], F32), ("sw2T", [2 * 128, 8 * D], F32),
            ("sw3T", [2 * 128, 8 * D], F32)],
}
STAGE_EMIT = {"l1f": emit_l1f, "l1a": emit_l1a, "l0f": emit_l0f, "l0a": emit_l0a}


def build_program(stages):
    nc = bass.Bass("TRN2", target_bir_lowering=False)
    es = ExitStack()
    need = [("xin", [D, NTOK], F32)] + const_inputs_spec()
    for st in stages:
        need += STAGE_INPUTS[st]
    with es:
        P = Prog(nc, es)
        c = make_ctx(nc, es, P, need)
        blk = es.enter_context(nc.Block())
        emit_load_stream(c)
        for st in stages:
            c.push_scope()
            STAGE_EMIT[st](c)
            c.pop_scope()
        emit_store_stream(c)
        P.finalize(blk)
    return nc


def tile128(a):
    a = np.ascontiguousarray(a)
    R, C = a.shape
    return np.ascontiguousarray(a.reshape(R // 128, 128, C).transpose(1, 0, 2).reshape(128, -1))


def host_inputs_for(stages, inputs):
    x = np.ascontiguousarray(inputs["L0_attn_x"])
    consts = const_inputs_host()
    maps = []
    stage_w = {}
    if "l0a" in stages:
        stage_w.update({
            "dwq": np.concatenate([tile128(np.asarray(inputs["L0_attn_Wq"])[:, h * 256:(h + 1) * 256]) for h in range(4)], axis=0).reshape(512, 8 * 256),
            "dwk": np.concatenate([tile128(np.asarray(inputs["L0_attn_Wk"])[:, h * 256:(h + 1) * 256]) for h in range(4)], axis=0).reshape(512, 8 * 256),
            "dwv": np.concatenate([tile128(np.asarray(inputs["L0_attn_Wv"])[:, h * 256:(h + 1) * 256]) for h in range(4)], axis=0).reshape(512, 8 * 256),
            "dwbeta": np.ascontiguousarray(np.asarray(inputs["L0_attn_Wbeta"]).reshape(8, 128, 4).transpose(1, 0, 2).reshape(128, 32)),
            "dwo": tile128(np.ascontiguousarray(inputs["L0_attn_Wo"])),
            "dgn": np.ascontiguousarray(np.asarray(inputs["L0_attn_g_norm"]).reshape(1, 256)),
        })
    if "l0f" in stages:
        stage_w.update({
            "gwT": np.ascontiguousarray(np.asarray(inputs["L0_ffn_gate_w"]).T.reshape(8, 128, 8).transpose(1, 0, 2).reshape(128, 64)),
            "b1c": np.concatenate([np.ascontiguousarray(
                np.asarray(inputs["L0_ffn_b1"][e]).reshape(8, 128).T) for e in range(8)], axis=1),
            "b2c": np.concatenate([np.ascontiguousarray(
                np.asarray(inputs["L0_ffn_b2"][e]).reshape(8, 128).T) for e in range(8)], axis=1),
            "b3r": np.ascontiguousarray(inputs["L0_ffn_b3"]),
            "w1T": np.concatenate([tile128(np.asarray(inputs["L0_ffn_w1"][e]).T)
                                   for e in range(8)], axis=0),
            "w2T": np.concatenate([tile128(np.asarray(inputs["L0_ffn_w2"][e]).T)
                                   for e in range(8)], axis=0),
            "w3T": np.concatenate([tile128(np.asarray(inputs["L0_ffn_w3"][e]).T)
                                   for e in range(8)], axis=0),
            "sw1T": np.concatenate([tile128(np.asarray(inputs["L0_ffn_sw1"]).T[:, i * 1024:(i + 1) * 1024])
                                    for i in range(2)], axis=0).reshape(256, 8 * D),
            "sw2T": np.concatenate([tile128(np.asarray(inputs["L0_ffn_sw2"]).T[:, i * 1024:(i + 1) * 1024])
                                    for i in range(2)], axis=0).reshape(256, 8 * D),
            "sw3T": np.concatenate([tile128(np.asarray(inputs["L0_ffn_sw3"]).T[i * 1024:(i + 1) * 1024, :])
                                    for i in range(2)], axis=0).reshape(256, 8 * D),
        })
    if "l1a" in stages:
        stage_w.update({
            "aqT": tile128(np.asarray(inputs["L1_attn_Wq"]).reshape(1024, 1024)),
            "akT": tile128(np.asarray(inputs["L1_attn_Wk"]).reshape(1024, 1024)),
            "avT": tile128(np.asarray(inputs["L1_attn_Wv"]).reshape(1024, 1024)),
            "aoT": tile128(np.asarray(inputs["L1_attn_Wout"]).reshape(1024, 1024)),
        })
    if "l1f" in stages:
        stage_w.update({
            "wrT": tile128(np.asarray(inputs["L1_ffn_W_r"]).T),
            "wkT": tile128(np.asarray(inputs["L1_ffn_W_k"]).T),
            "wvT": tile128(np.asarray(inputs["L1_ffn_W_v"]).T),
            "mur": np.ascontiguousarray(inputs["L1_ffn_mu_r"].reshape(8, 128).T),
            "muk": np.ascontiguousarray(inputs["L1_ffn_mu_k"].reshape(8, 128).T),
        })
    for b in range(N_CORES):
        m = {"xin": np.ascontiguousarray(x[b].T)}
        m.update(consts)
        m.update(stage_w)
        maps.append(m)
    return maps


def kernel(**inputs):
    stages = ["l0a", "l0f", "l1a", "l1f"]
    nc = build_program(stages)
    maps = host_inputs_for(stages, inputs)
    res = run_bass_kernel_spmd(nc, maps, core_ids=list(range(N_CORES)))
    out = np.stack([np.ascontiguousarray(r["out"].T) for r in res.results], axis=0)
    return out.astype(np.float32)
